# version 12
# baseline (speedup 1.0000x reference)
import numpy as np
from contextlib import ExitStack

import concourse.bass as bass
import concourse.mybir as mybir
from concourse.bass_utils import run_bass_kernel_spmd

F32 = mybir.dt.float32
BF16 = mybir.dt.bfloat16
AF = mybir.ActivationFunctionType
ALU = mybir.AluOpType
AP = bass.AP

B, T, DM = 4, 4096, 1024
H = 16
NOWN = 2048
CH = 256
NCH = NOWN // CH
DB = 8
EPS = 1e-5
E20 = float(np.exp(np.float32(20.0)))
EM20 = float(np.exp(np.float32(-20.0)))
INV_SQRT32 = float(np.float32(1.0) / np.float32(np.sqrt(np.float32(32.0))))

ENGINES = ("tensor", "vector", "scalar", "sync", "gpsimd")
DMA_SEMS = ("din", "dw", "dxa", "dwc", "dout", "ddbg")
CHUNK_UPTO = 99
KFT_EVAC = 1
FFN_UPTO = 99


class Prog:

    def __init__(self, semnames):
        self.q = {e: [] for e in ENGINES}
        self.cnt = {s: 0 for s in semnames}
        self.sems = None
        self._seq = []
        self._sno = 0

    def emit(self, eng, fn, waits=(), inc=None, inc_n=1):
        self._seq.append((self._sno, eng, len(self.q[eng])))
        self._sno += 1
        self.q[eng].append((tuple(waits), fn, inc, inc_n))
        if inc is not None:
            self.cnt[inc] += inc_n

    def snap(self, sem):
        return self.cnt[sem]

    def run(self, nc, sems):
        self.sems = sems
        prog = self
        waited = {s: set() for s in self.cnt}
        for elist in self.q.values():
            for waits, fn, inc, inc_n in elist:
                for s, v in waits:
                    if v > 0:
                        waited[s].add(v)
        rank = {}
        for s, vals in waited.items():
            rank[s] = {}
            for i, v in enumerate(sorted(vals)):
                rank[s][v] = i + 1
        raw = {s: 0 for s in self.cnt}
        seq = sorted(self._seq, key=lambda t: t[0])
        signal_for = {}
        for sno, e, idx in seq:
            waits, fn, inc, inc_n = self.q[e][idx]
            if inc is not None:
                raw[inc] += inc_n
                signal_for[(e, idx)] = True
        with nc.Block() as blk:
            def body(engobj, ename, elist):
                deep = engobj is nc.vector or engobj is nc.scalar
                for idx, (waits, fn, inc, inc_n) in enumerate(elist):
                    for s, v in waits:
                        if v > 0:
                            engobj.wait_ge(prog.sems[s], v)
                    ins = fn(engobj)
                    if inc is not None and signal_for.get((ename, idx), False):
                        if deep:
                            engobj.drain(fusable=True).then_inc(
                                prog.sems[inc], inc_n)
                        else:
                            ins.then_inc(prog.sems[inc], inc_n)
            blk.tensor(lambda e: body(e, "tensor", prog.q["tensor"]))
            blk.vector(lambda e: body(e, "vector", prog.q["vector"]))
            blk.scalar(lambda e: body(e, "scalar", prog.q["scalar"]))
            blk.sync(lambda e: body(e, "sync", prog.q["sync"]))
            blk.gpsimd(lambda e: body(e, "gpsimd", prog.q["gpsimd"]))


def build_nc(debug=(), n_carry=NCH, n_main=NCH, do_ffn=True, dumps=()):
    nc = bass.Bass("TRN2", target_bir_lowering=False)
    dbg = {}

    din = {}
    def dram_in(name, shape):
        din[name] = nc.dram_tensor(name, list(shape), F32, kind="ExternalInput")
        return din[name]

    x_own = dram_in("x_own", (DM, NOWN))
    x_carry = dram_in("x_carry", (DM, NOWN))
    cscale = dram_in("cscale", (128, 1))
    wqkvT = dram_in("wqkvT", (DM, 3 * DM))
    bqkv_c = dram_in("bqkv_c", (128, 16))
    bqv_row = dram_in("bqv_row", (1, DM))
    woutT = dram_in("woutT", (DM, DM))
    bout_c = dram_in("bout_c", (128, 8))
    omega2 = dram_in("omega2", (64, 1024))
    qnw_col = dram_in("qnw_col", (128, 2))
    rmt = dram_in("rmt", (16, 4096))
    rut = dram_in("rut", (16, 16384))
    wcT = dram_in("wcT", (16384, DM))
    c_eye = dram_in("c_eye", (128, 128))
    c_tri = dram_in("c_tri", (128, 128))
    c_allones = dram_in("c_allones", (128, 128))
    c_blockones = dram_in("c_blockones", (128, 128))
    c_diagmask = dram_in("c_diagmask", (32, 128))
    c_mask384 = dram_in("c_mask384", (128, 384))

    y_out = nc.dram_tensor("y_out", [DM, NOWN], F32, kind="ExternalOutput")
    for name, shape in debug:
        dbg[name] = nc.dram_tensor(name, list(shape), F32, kind="ExternalOutput")

    es = ExitStack()
    with es:
        sb = lambda n, shp, dt: es.enter_context(nc.sbuf_tensor(n, list(shp), dt))
        x_sb = sb("x_sb", (128, DB * NOWN), F32)
        railsM = sb("railsM", (128, 32 * 128), BF16)
        mask384_bf = railsM
        omega_bf = sb("omega_bf", (128, 1024), BF16)
        eye_bf = sb("eye_bf", (128, 128), BF16)
        allones_bf = sb("allones_bf", (128, 128), BF16)
        blockones_bf = sb("blockones_bf", (128, 128), BF16)
        diagmask_bf = sb("diagmask_bf", (32, 128), BF16)
        bqkv_sb = sb("bqkv_sb", (128, 16), F32)
        bout_sb = sb("bout_sb", (128, 8), F32)
        svecs = sb("svecs", (128, 4), F32)
        cscale_sb = sb("cscale_sb", (128, 1), F32)
        qnw_sb = sb("qnw_sb", (128, 2), F32)
        small_f = sb("small_f", (128, 4), F32)
        eps_c = sb("eps_c", (128, 1), F32)
        nc.const_aps.register(F32, EPS, eps_c[:, :])

        sems = {}
        for s in ("pe", "dv", "ac", "din", "dw", "dxa", "dwc", "dout", "ddbg"):
            sems[s] = es.enter_context(nc.semaphore("sem_" + s))

        P = Prog(list(sems))

        def T_(fn, waits=()):
            P.emit("tensor", fn, waits, "pe")

        def V_(fn, waits=()):
            P.emit("vector", fn, waits, "dv")

        def A_(fn, waits=()):
            P.emit("scalar", fn, waits, "ac")

        def SY(fn, waits=(), inc=None):
            P.emit("sync", fn, waits, inc, 16)

        def GP(fn, waits=(), inc=None):
            P.emit("gpsimd", fn, waits, inc, 16)

        def ap3(t, off=0, w=None, blocks=DB, cw=256):
            W = t.shape[1]
            return AP(t, off, [[W, 128], [cw if w is None else w, blocks], [1, cw]])

        for sbuf, dr in ((allones_bf, c_allones),
                         (blockones_bf, c_blockones), (eye_bf, c_eye),
                         (diagmask_bf, c_diagmask)):
            GP((lambda g, a=sbuf, b=dr: g.dma_start(a[:, :], b[:, :])), inc="dw")
        GP(lambda g: g.dma_start(mask384_bf[:, 0:384], c_mask384[:, :]), inc="dw")
        GP(lambda g: g.dma_start(railsM[0:1, 384:1408], bqv_row[:, :]), inc="dw")
        GP(lambda g: g.dma_start(omega_bf[0:64, :], omega2[:, :]), inc="dw")
        GP(lambda g: g.dma_start(omega_bf[64:128, :], omega2[:, :]), inc="dw")
        for sbuf, dr in ((bqkv_sb, bqkv_c),
                         (bout_sb, bout_c), (qnw_sb, qnw_col),
                         (cscale_sb, cscale)):
            SY((lambda s, a=sbuf, b=dr: s.dma_start(a[:, :], b[:, :])), inc="din")
        for db in range(DB):
            SY((lambda s, db=db: s.dma_start(
                AP(x_sb, db * NOWN, [[DB * NOWN, 128], [1, NOWN]]),
                AP(x_own, db * 128 * NOWN, [[NOWN, 128], [1, NOWN]]))), inc="din")
        n_din0 = P.snap("din")

        psA = es.enter_context(nc.psum_tensor("psA", [128, 1024], F32))
        psB = es.enter_context(nc.psum_tensor("psB", [128, 2048], F32))
        psT = es.enter_context(nc.psum_tensor("psT", [128, 2048], BF16))

        V_(lambda v: v.memset(railsM[0:1, 1408:1536], 1.0))
        V_(lambda v: v.memset(eps_c[:, :], EPS))
        V_(lambda v: v.tensor_scalar(small_f[:, 0:1], qnw_sb[:, 0:1], 0.0, None,
                                     op0=ALU.max), waits=[("din", n_din0)])
        V_(lambda v: v.tensor_scalar(small_f[:, 1:2], small_f[:, 0:1], 2.0, 1e-30,
                                     op0=ALU.mult, op1=ALU.max))
        d_sv = P.snap("dv")
        A_(lambda a: a.activation(small_f[:, 2:3], small_f[:, 1:2], AF.Ln),
           waits=[("dv", d_sv)])
        A_(lambda a: a.activation(svecs[:, 0:1], small_f[:, 2:3], AF.Exp, scale=0.5))
        V_(lambda v: v.tensor_scalar(svecs[:, 1:2], qnw_sb[:, 0:1], -1.0, None,
                                     op0=ALU.mult))
        V_(lambda v: v.tensor_scalar(small_f[:, 3:4], qnw_sb[:, 1:2], 0.0, 1e-30,
                                     op0=ALU.max, op1=ALU.max))
        d_sv2 = P.snap("dv")
        A_(lambda a: a.activation(small_f[:, 0:1], small_f[:, 3:4], AF.Ln),
           waits=[("dv", d_sv2)])
        A_(lambda a: a.activation(small_f[:, 1:2], small_f[:, 0:1], AF.Exp,
                                  scale=0.25))
        V_(lambda v: v.tensor_scalar(svecs[:, 2:3], small_f[:, 1:2], INV_SQRT32,
                                     None, op0=ALU.mult),
           waits=[("ac", P.snap("ac"))])

        xa_bf = sb("xa_bf", (128, 2048), BF16)
        sq_bf = sb("sq_bf", (128, 2048), BF16)
        pn_bf = sb("pn_bf", (128, 2048), BF16)
        f32s = sb("f32s", (128, 1024), F32)
        mr_bf = sb("mr_bf", (128, 512), BF16)
        es_attn = ExitStack()
        sba = lambda n, shp, dt: es_attn.enter_context(nc.sbuf_tensor(n, list(shp), dt))
        wqkv_bf = sba("wqkv_bf", (128, DB * 3072), BF16)
        wout_bf = sba("wout_bf", (128, DB * 1024), BF16)
        for db in range(DB):
            GP((lambda g, db=db: g.dma_start(
                AP(wqkv_bf, db * 3072, [[DB * 3072, 128], [1, 3072]]),
                AP(wqkvT, db * 128 * 3072, [[3072, 128], [1, 3072]]))), inc="dw")
        for db in range(DB):
            GP((lambda g, db=db: g.dma_start(
                AP(wout_bf, db * 1024, [[DB * 1024, 128], [1, 1024]]),
                AP(woutT, db * 128 * 1024, [[1024, 128], [1, 1024]]))), inc="dw")
        n_dw0 = P.snap("dw")
        ln1_bf = sba("ln1_bf", (128, 2048), BF16)
        qk_bf = sba("qk_bf", (128, 2048), BF16)
        qf2 = [sba("qf_a", (128, 2048), BF16), sba("qf_b", (128, 2048), BF16)]
        kf2 = [sba("kf_a", (128, 2048), BF16), sba("kf_b", (128, 2048), BF16)]
        kT2 = [sba("kT_a", (128, 2048), BF16), sba("kT_b", (128, 2048), BF16)]
        va2 = [sba("va_a", (128, 2 * 1056), BF16), sba("va_b", (128, 2 * 1056), BF16)]
        ctx_bf = sba("ctx_bf", (128, 2048), BF16)
        ab_bf = railsM[:, 3424:3808]
        S_f = sba("S_f", (128, 8 * 66), F32)
        S_bf = railsM[:, 1856:2384]
        denrep = railsM[:, 2384:2896]
        dinv = railsM[:, 1536:1792]
        f32sB = sba("f32sB", (128, 1024), F32)
        mrB = sba("mrB", (128, 512), BF16)

        for par in range(2):
            for sub in range(2):
                V_((lambda v, par=par, sub=sub: v.memset(
                    AP(va2[par], sub * 1056 + 64,
                       [[2 * 1056, 128], [66, 16], [1, 1]]), 1.0)))
        V_(lambda v: v.memset(S_f[:, :], 0.0))
        V_(lambda v: v.memset(S_bf[:, :], 0.0))

        def emit_ln(xb, out_buf, ps, scr_sq, scr_f, scr_mr, first_waits=(),
                    resid_cols=None):
            d0 = P.snap("dv")
            pe0 = P.snap("pe")
            A_((lambda a: a.activation(scr_sq[:, :], xb[:, :], AF.Square)),
               waits=list(first_waits) + [("dv", d0), ("pe", pe0)])
            a0 = P.snap("ac")
            for db in range(DB):
                T_((lambda t, db=db: t.matmul(
                    ps[:, 0:256], allones_bf[:, :],
                    xb[:, db * 256:(db + 1) * 256],
                    start=(db == 0), stop=(db == DB - 1))),
                   waits=(list(first_waits) + [("dv", d0)]) if db == 0 else [])
            for db in range(DB):
                T_((lambda t, db=db: t.matmul(
                    ps[:, 256:512], allones_bf[:, :],
                    scr_sq[:, db * 256:(db + 1) * 256],
                    start=(db == 0), stop=(db == DB - 1))),
                   waits=[("ac", a0)] if db == 0 else [])
            p0 = P.snap("pe")
            V_((lambda v: v.tensor_scalar(scr_f[:, 0:256], ps[:, 0:256], 1.0 / DM,
                                          None, op0=ALU.mult)), waits=[("pe", p0)])
            V_((lambda v: v.tensor_tensor(scr_f[:, 256:512], scr_f[:, 0:256],
                                          scr_f[:, 0:256], op=ALU.mult)))
            V_((lambda v: v.scalar_tensor_tensor(scr_f[:, 512:768], ps[:, 256:512],
                                                 1.0 / DM, scr_f[:, 256:512],
                                                 op0=ALU.mult, op1=ALU.subtract)))
            V_((lambda v: v.tensor_copy(scr_mr[:, 0:256], scr_f[:, 0:256])))
            d1 = P.snap("dv")
            A_((lambda a: a.activation(scr_f[:, 768:1024], scr_f[:, 512:768], AF.Ln,
                                       bias=EPS)), waits=[("dv", d1)])
            A_((lambda a: a.activation(scr_mr[:, 256:512], scr_f[:, 768:1024],
                                       AF.Exp, scale=-0.5)))
            a1 = P.snap("ac")
            rep_m = AP(scr_mr, 0, [[512, 128], [0, 8], [1, 256]])
            rep_r = AP(scr_mr, 256, [[512, 128], [0, 8], [1, 256]])
            xb3 = ap3(xb)
            ob3 = ap3(out_buf)
            V_((lambda v: v.tensor_tensor(ob3, xb3, rep_m, op=ALU.subtract)),
               waits=[("ac", a1)])
            V_((lambda v: v.tensor_tensor(ob3, ob3, rep_r, op=ALU.mult)))
            if resid_cols is not None:
                xap = AP(x_sb, resid_cols, [[DB * NOWN, 128], [NOWN, 8], [1, 256]])
                V_((lambda v: v.tensor_tensor(xap, xap, ob3, op=ALU.add)))

        def emit_features(src_bf, out_bf, is_carry):
            d0 = P.snap("dv")
            pe0 = P.snap("pe")
            A_((lambda a: a.activation(sq_bf[:, :], src_bf[:, :], AF.Square)),
               waits=[("dv", d0), ("pe", pe0)])
            a0 = P.snap("ac")
            for half in range(2):
                c0 = half * 1024
                dvp = P.snap("dv")
                for bb4 in range(4):
                    bb = half * 4 + bb4
                    T_((lambda t, bb=bb, bb4=bb4: t.matmul(
                        psA[:, bb4 * 256:(bb4 + 1) * 256], blockones_bf[:, :],
                        sq_bf[:, bb * 256:(bb + 1) * 256], start=True, stop=True)),
                       waits=[("ac", a0), ("dv", dvp)] if bb4 == 0 else [])
                p0 = P.snap("pe")
                V_((lambda v: v.tensor_scalar(
                    f32s[:, 0:1024], psA[:, 0:1024], 1e-24,
                    None, op0=ALU.max)), waits=[("pe", p0)])
                d1 = P.snap("dv")
                A_((lambda a: a.activation(f32s[:, 0:1024], f32s[:, 0:1024],
                                           AF.Ln)), waits=[("dv", d1)])
                A_((lambda a, c0=c0: a.activation(
                    xa_bf[:, c0:c0 + 1024], f32s[:, 0:1024], AF.Exp, scale=-0.5)))
                a1 = P.snap("ac")
                V_((lambda v, c0=c0: v.tensor_scalar(
                    xa_bf[:, c0:c0 + 1024], xa_bf[:, c0:c0 + 1024], 1e12, None,
                    op0=ALU.min)), waits=[("ac", a1)])
                inv_done = P.snap("dv")
                for bb4 in range(4):
                    bb = half * 4 + bb4
                    for hh in range(2):
                        hcur = 2 * bb + hh
                        lo = 64 * hh
                        kw = {} if hh == 0 else {"tile_position": (64, 64)}
                        T_((lambda t, bb=bb, bb4=bb4, lo=lo, hcur=hcur, kw=kw:
                            t.matmul(
                                psA[lo:lo + 64, bb4 * 256:(bb4 + 1) * 256],
                                omega_bf[lo:lo + 64, hcur * 64:(hcur + 1) * 64],
                                src_bf[lo:lo + 64, bb * 256:(bb + 1) * 256],
                                start=True, stop=True, **kw)),
                           waits=[("dv", inv_done)] if (bb4 == 0 and hh == 0) else [])
                p1 = P.snap("pe")
                V_((lambda v, c0=c0: v.tensor_tensor(
                    pn_bf[:, c0:c0 + 1024], psA[:, 0:1024],
                    xa_bf[:, c0:c0 + 1024], op=ALU.mult)), waits=[("pe", p1)])
            ret = P.snap("dv")
            d2 = P.snap("dv")
            pe2 = P.snap("pe")
            A_((lambda a: a.activation(sq_bf[:, :], pn_bf[:, :], AF.Exp,
                                       bias=svecs[:, 1:2], scale=svecs[:, 0:1])),
               waits=[("dv", d2), ("pe", pe2)])
            A_((lambda a: a.activation(xa_bf[:, :], pn_bf[:, :], AF.Square,
                                       scale=svecs[:, 2:3])))
            a2 = P.snap("ac")
            V_((lambda v: v.tensor_scalar(pn_bf[:, :], sq_bf[:, :], E20, EM20,
                                          op0=ALU.min, op1=ALU.max)),
               waits=[("ac", a2)])
            if is_carry:
                V_((lambda v: v.scalar_tensor_tensor(out_bf[:, :], xa_bf[:, :],
                                                     cscale_sb[:, :], pn_bf[:, :],
                                                     op0=ALU.mult, op1=ALU.mult)))
            else:
                V_((lambda v: v.tensor_tensor(out_bf[:, :], xa_bf[:, :],
                                              pn_bf[:, :], op=ALU.mult)))
            return ret

        marks = {}

        def stageA_gen(ci, carry, parity):
            qfb, kfb, kTb, vab = qf2[parity], kf2[parity], kT2[parity], va2[parity]
            if carry:
                GP((lambda g, ci=ci: g.dma_start(
                    AP(xa_bf, 0, [[2048, 128], [256, 8], [1, 256]]),
                    AP(x_carry, ci * 256, [[NOWN, 128], [128 * NOWN, 8], [1, 256]]))),
                   waits=[("dv", P.snap("dv"))], inc="dxa")
                fw = [("dxa", P.snap("dxa"))]
            else:
                A_((lambda a, ci=ci: a.activation(
                    ap3(xa_bf),
                    AP(x_sb, ci * 256, [[DB * NOWN, 128], [NOWN, 8], [1, 256]]),
                    AF.Copy)), waits=[("din", n_din0)])
                fw = [("ac", P.snap("ac"))]
            fw = fw + [("dw", n_dw0)]
            yield
            emit_ln(xa_bf, ln1_bf, psA, sq_bf, f32s, mr_bf, first_waits=fw)
            ln_done = P.snap("dv")
            yield

            def qkv_pass(jb0, dst, dstoff, first_w):
                for jj in range(4):
                    jb = jb0 + jj
                    for db in range(DB):
                        T_((lambda t, jj=jj, jb=jb, db=db: t.matmul(
                            psA[:, jj * 256:(jj + 1) * 256],
                            wqkv_bf[:, db * 3072 + jb * 128:db * 3072 + (jb + 1) * 128],
                            ln1_bf[:, db * 256:(db + 1) * 256],
                            start=(db == 0), stop=(db == DB - 1))),
                           waits=list(first_w) + [("dw", n_dw0)]
                           if (jj == 0 and db == 0) else [])
                pp = P.snap("pe")
                for jj in range(4):
                    A_((lambda a, dst=dst, dstoff=dstoff, jj=jj, jb0=jb0:
                        a.activation(
                            dst[:, dstoff + jj * 256:dstoff + (jj + 1) * 256],
                            psA[:, jj * 256:(jj + 1) * 256], AF.Identity,
                            bias=bqkv_sb[:, jb0 + jj:jb0 + jj + 1])),
                       waits=[("pe", pp)] if jj == 0 else [])
                return P.snap("ac")

            free = [("dv", ln_done)]
            if not carry:
                for pp_ in range(2):
                    free = [("ac", qkv_pass(pp_ * 4, qk_bf, pp_ * 1024, free))]
                    yield
                free = [("dv", emit_features(qk_bf, qfb, False))]
                yield
            for pp_ in range(2):
                free = [("ac", qkv_pass(8 + pp_ * 4, qk_bf, pp_ * 1024, free))]
                yield
            free = [("dv", emit_features(qk_bf, kfb, carry))]
            kf_done = P.snap("dv")
            yield
            for sub in range(2):
                for db in range(DB):
                    for oh in range(2):
                        T_((lambda t, sub=sub, db=db, oh=oh: t.matmul(
                            psA[:, oh * 512:(oh + 1) * 512],
                            ln1_bf[:, db * 256 + sub * 128:db * 256 + sub * 128 + 128],
                            wqkv_bf[:, db * 3072 + 2048 + oh * 512:
                                    db * 3072 + 2048 + (oh + 1) * 512],
                            start=(db == 0), stop=False)),
                           waits=list(free) if (db == 0 and oh == 0) else [])
                for oh in range(2):
                    T_((lambda t, oh=oh: t.matmul(
                        psA[:, oh * 512:(oh + 1) * 512],
                        railsM[0:1, 1408:1536],
                        railsM[0:1, 384 + oh * 512:384 + (oh + 1) * 512],
                        start=False, stop=True)))
                pv = P.snap("pe")
                A_((lambda a, sub=sub, vab=vab: a.activation(
                    AP(vab, sub * 1056, [[2 * 1056, 128], [66, 16], [1, 64]]),
                    AP(psA, 0, [[1024, 128], [64, 16], [1, 64]]), AF.Copy)),
                   waits=[("pe", pv)])
                free = [("ac", P.snap("ac"))]
                yield
            def ftr(t, kfb=kfb):
                last = None
                for i in range(16):
                    if i:
                        t.nop()
                    last = t.transpose(psT[:, i * 128:(i + 1) * 128],
                                       kfb[:, i * 128:(i + 1) * 128],
                                       eye_bf[:, :])
                return last
            T_(ftr, waits=[("dv", kf_done)])
            pt = P.snap("pe")
            A_((lambda a, kTb=kTb: a.activation(kTb[:, :], psT[:, :], AF.Copy)),
               waits=[("pe", pt)])
            marks[ci] = dict(kfT_done_ac=P.snap("ac"))
            yield
            if carry:
                for hh in range(H):
                    lo = (hh % 2) * 64
                    bb = hh // 2
                    off = hh * 128
                    kw = {} if hh % 2 == 0 else {"tile_position": (0, 64)}
                    for sub in range(2):
                        T_((lambda t, hh=hh, lo=lo, bb=bb, off=off, sub=sub,
                            kw=kw, kTb=kTb, vab=vab: t.matmul(
                                psB[lo:lo + 64, off:off + 65],
                                kTb[:, (bb * 2 + sub) * 128 + lo:
                                    (bb * 2 + sub) * 128 + lo + 64],
                                vab[:, sub * 1056 + 66 * hh:sub * 1056 + 66 * hh + 65],
                                start=(sub == 0), stop=(sub == 1), **kw)),
                           waits=[("ac", marks[ci]["kfT_done_ac"])]
                           if (hh == 0 and sub == 0) else [])
                pS = P.snap("pe")
                for hh in range(H):
                    lo = (hh % 2) * 64
                    off = hh * 128
                    V_((lambda v, hh=hh, lo=lo, off=off: v.tensor_tensor(
                        S_f[lo:lo + 64, 66 * (hh // 2):66 * (hh // 2) + 65],
                        S_f[lo:lo + 64, 66 * (hh // 2):66 * (hh // 2) + 65],
                        psB[lo:lo + 64, off:off + 65], op=ALU.add)),
                       waits=[("pe", pS)] if hh == 0 else [])
                    V_((lambda v, hh=hh, lo=lo: v.tensor_copy(
                        S_bf[lo:lo + 64, 66 * (hh // 2):66 * (hh // 2) + 65],
                        S_f[lo:lo + 64, 66 * (hh // 2):66 * (hh // 2) + 65])))

        def emit_stageB(ci, parity, pump):
            qfb, kfb, kTb, vab = qf2[parity], kf2[parity], kT2[parity], va2[parity]
            kfT_done_ac = marks[ci]["kfT_done_ac"]
            V_((lambda v: v.tensor_copy(
                AP(railsM, 2384, [[4096, 128], [64, 8], [1, 64]]),
                AP(railsM, 1856 + 64, [[4096, 128], [66, 8], [0, 64]]))),
               waits=[("pe", P.snap("pe"))])
            slot_dv = [P.snap("dv")] * 2
            for hh in range(H):
                if hh % 2 == 0:
                    pump()
                sl = hh % 2
                base = sl * 1024
                bb, lo = hh // 2, (hh % 2) * 64
                kwo = {} if sl == 0 else {"tile_position": (0, 64)}
                kws = {} if sl == 0 else {"tile_position": (64, 64)}
                kwa = {} if sl == 0 else {"tile_position": (64, 0)}
                ctx0 = base + 512
                den0 = base + 768
                dS0 = base + 384

                T_((lambda t, base=base, bb=bb, lo=lo, kwa=kwa, kfb=kfb, qfb=qfb:
                    t.matmul(
                        psB[:, base:base + 256],
                        kfb[lo:lo + 64, bb * 256:bb * 256 + 128],
                        qfb[lo:lo + 64, bb * 256:bb * 256 + 256],
                        start=True, stop=True, **kwa)),
                   waits=[("dv", slot_dv[sl]), ("ac", kfT_done_ac)])
                T_((lambda t, base=base, bb=bb, lo=lo, kwa=kwa, kfb=kfb, qfb=qfb:
                    t.matmul(
                        psB[:, base + 256:base + 384],
                        kfb[lo:lo + 64, bb * 256 + 128:bb * 256 + 256],
                        qfb[lo:lo + 64, bb * 256 + 128:bb * 256 + 256],
                        start=True, stop=True, **kwa)))
                pa = P.snap("pe")
                V_((lambda v, base=base: v.tensor_tensor(
                    ab_bf[:, 0:384], psB[:, base:base + 384],
                    mask384_bf[:, 0:384], op=ALU.mult)), waits=[("pe", pa)])
                da = P.snap("dv")

                T_((lambda t, ctx0=ctx0, hh=hh, lo=lo, bb=bb, kws=kws, qfb=qfb:
                    t.matmul(
                        psB[lo:lo + 64, ctx0:ctx0 + 256],
                        S_bf[lo:lo + 64, 66 * (hh // 2):66 * (hh // 2) + 64],
                        qfb[lo:lo + 64, bb * 256:bb * 256 + 256],
                        start=True, stop=False, **kws)), waits=[("dv", da)])
                T_((lambda t, ctx0=ctx0, hh=hh, lo=lo, kwo=kwo, vab=vab: t.matmul(
                    psB[lo:lo + 64, ctx0:ctx0 + 256],
                    vab[:, 66 * hh:66 * hh + 64],
                    ab_bf[:, 0:256], start=False, stop=False, **kwo)))
                T_((lambda t, ctx0=ctx0, hh=hh, lo=lo, kwo=kwo, vab=vab: t.matmul(
                    psB[lo:lo + 64, ctx0 + 128:ctx0 + 256],
                    vab[:, 1056 + 66 * hh:1056 + 66 * hh + 64],
                    ab_bf[:, 256:384], start=False, stop=True, **kwo)))
                T_((lambda t, den0=den0, hh=hh, lo=lo, bb=bb, kws=kws,
                    qfb=qfb: t.matmul(
                        psB[lo:lo + 64, den0:den0 + 256],
                        denrep[lo:lo + 64, bb * 64:bb * 64 + 64],
                        qfb[lo:lo + 64, bb * 256:bb * 256 + 256],
                        start=True, stop=False, **kws)))
                T_((lambda t, den0=den0, lo=lo, kwo=kwo: t.matmul(
                    psB[lo:lo + 64, den0:den0 + 256],
                    allones_bf[:, 0:64],
                    ab_bf[:, 0:256], start=False, stop=False, **kwo)))
                T_((lambda t, den0=den0, lo=lo, kwo=kwo: t.matmul(
                    psB[lo:lo + 64, den0 + 128:den0 + 256],
                    allones_bf[:, 0:64],
                    ab_bf[:, 256:384], start=False, stop=True, **kwo)))
                T_((lambda t, dS0=dS0, hh=hh, lo=lo, bb=bb, kwo=kwo, kTb=kTb,
                    vab=vab: t.matmul(
                        psB[lo:lo + 64, dS0:dS0 + 65],
                        kTb[:, (bb * 2) * 128 + lo:(bb * 2) * 128 + lo + 64],
                        vab[:, 66 * hh:66 * hh + 65],
                        start=True, stop=False, **kwo)))
                T_((lambda t, dS0=dS0, hh=hh, lo=lo, bb=bb, kwo=kwo, kTb=kTb,
                    vab=vab: t.matmul(
                        psB[lo:lo + 64, dS0:dS0 + 65],
                        kTb[:, (bb * 2 + 1) * 128 + lo:(bb * 2 + 1) * 128 + lo + 64],
                        vab[:, 1056 + 66 * hh:1056 + 66 * hh + 65],
                        start=False, stop=True, **kwo)))
                pc = P.snap("pe")

                V_((lambda v, den0=den0, lo=lo: v.tensor_scalar(
                    f32sB[lo:lo + 64, 0:256], psB[lo:lo + 64, den0:den0 + 256],
                    1e-6, None, op0=ALU.max)), waits=[("pe", pc)])
                dd = P.snap("dv")
                A_((lambda a, lo=lo: a.activation(
                    f32sB[lo:lo + 64, 0:256], f32sB[lo:lo + 64, 0:256], AF.Ln)),
                   waits=[("dv", dd)])
                A_((lambda a, lo=lo: a.activation(
                    dinv[lo:lo + 64, 0:256], f32sB[lo:lo + 64, 0:256], AF.Exp,
                    scale=-1.0)))
                ad = P.snap("ac")
                V_((lambda v, ctx0=ctx0, lo=lo, bb=bb: v.tensor_tensor(
                    ctx_bf[lo:lo + 64, bb * 256:(bb + 1) * 256],
                    psB[lo:lo + 64, ctx0:ctx0 + 256],
                    dinv[lo:lo + 64, 0:256], op=ALU.mult)), waits=[("ac", ad)])
                V_((lambda v, dS0=dS0, hh=hh, lo=lo: v.tensor_tensor(
                    S_f[lo:lo + 64, 66 * (hh // 2):66 * (hh // 2) + 65],
                    S_f[lo:lo + 64, 66 * (hh // 2):66 * (hh // 2) + 65],
                    psB[lo:lo + 64, dS0:dS0 + 65], op=ALU.add)))
                slot_dv[sl] = P.snap("dv")
            V_((lambda v: v.tensor_copy(S_bf[:, :], S_f[:, :])))
            scan_done = P.snap("dv")

            pump()
            for ob in range(8):
                for db in range(DB):
                    T_((lambda t, ob=ob, db=db: t.matmul(
                        psB[:, ob * 256:(ob + 1) * 256],
                        wout_bf[:, db * 1024 + ob * 128:db * 1024 + (ob + 1) * 128],
                        ctx_bf[:, db * 256:(db + 1) * 256],
                        start=(db == 0), stop=(db == DB - 1))),
                       waits=[("dv", scan_done)] if (ob == 0 and db == 0) else [])
            po = P.snap("pe")
            for ob in range(8):
                V_((lambda v, ob=ob, qfb=qfb: v.tensor_scalar(
                    qfb[:, ob * 256:(ob + 1) * 256],
                    psB[:, ob * 256:(ob + 1) * 256],
                    bout_sb[:, ob:ob + 1], None, op0=ALU.add)),
                   waits=[("pe", po)] if ob == 0 else [])
            pump()
            emit_ln(qfb, ctx_bf, psB, kfb, f32sB, mrB, resid_cols=ci * 256)
            pump()

        bufs = dict(x_sb=x_sb, svecs=svecs, ln1_bf=ln1_bf,
                    qk_bf=qk_bf, vaug=va2[0], qf_bf=qf2[0],
                    kf_bf=kf2[0], ctx=ctx_bf,
                    S_f=S_f, S_bf=S_bf, xa_bf=xa_bf, sq_bf=sq_bf, pn_bf=pn_bf,
                    wqkv_bf=wqkv_bf, omega_bf=omega_bf, mr_bf=mr_bf, f32s=f32s,
                    small_f=small_f, qnw_sb=qnw_sb)

        for ci in range(n_carry):
            for _ in stageA_gen(ci, True, 0):
                pass
        pend = []

        def pump():
            if pend:
                gen = pend[0]
                try:
                    next(gen)
                except StopIteration:
                    pend.pop(0)

        if n_main > 0:
            g0 = stageA_gen(0, False, 0)
            for _ in g0:
                pass
        for i in range(n_main):
            if i + 1 < n_main:
                pend.append(stageA_gen(i + 1, False, (i + 1) % 2))
            emit_stageB(i, i % 2, pump)
            while pend:
                pump()
        attn_done_dv = P.snap("dv")

        for bn in dumps:
            bt = bufs[bn]
            rows, cols = bt.shape
            dt_out = nc.dram_tensor("dump_" + bn, [rows, cols], F32,
                                    kind="ExternalOutput")
            dbg["dump_" + bn] = dt_out
            GP((lambda g, bt=bt, dt_out=dt_out, rows=rows, cols=cols: g.dma_start(
                AP(dt_out, 0, [[cols, rows], [1, cols]]),
                AP(bt, 0, [[cols, rows], [1, cols]]))),
               waits=[("dv", P.snap("dv")), ("ac", P.snap("ac")),
                      ("pe", P.snap("pe"))], inc="ddbg")

        if dbg.get("x_attn") is not None:
            SY((lambda s: s.dma_start(
                AP(dbg["x_attn"], 0, [[NOWN, 128], [NOWN * 128, 8], [1, NOWN]]),
                AP(x_sb, 0, [[DB * NOWN, 128], [NOWN, 8], [1, NOWN]]))),
               waits=[("dv", attn_done_dv)], inc="dout")

        if not do_ffn:
            SY((lambda s: s.dma_start(
                AP(y_out, 0, [[NOWN, 128], [NOWN * 128, 8], [1, NOWN]]),
                AP(x_sb, 0, [[DB * NOWN, 128], [NOWN, 8], [1, NOWN]]))),
               waits=[("dv", attn_done_dv)], inc="dout")
            n_out = P.snap("dout")
            SY(lambda s: s.nop(), waits=[("dout", n_out)])
            GP(lambda g: g.nop(), waits=[("ddbg", P.snap("ddbg"))])
            P.run(nc, sems)
            es_attn.close()
            return nc, dbg
        es_attn.close()
        with nc.sbuf_tensor("rmt_bf", [16, 4096], BF16) as rmt_bf, \
             nc.sbuf_tensor("rut_bf", [16, 16384], BF16) as rut_bf:
            GP(lambda g: g.dma_start(rmt_bf[:, :], rmt[:, :]),
               waits=[("dv", attn_done_dv)], inc="dw")
            for part in range(4):
                GP((lambda g, part=part: g.dma_start(
                    rut_bf[:, part * 4096:(part + 1) * 4096],
                    AP(rut, part * 4096, [[16384, 16], [1, 4096]]))), inc="dw")
            n_dw1 = P.snap("dw")
            V_(lambda v: v.memset(railsM[:, :], 0.0),
               waits=[("dv", attn_done_dv)])
            n_zero = P.snap("dv")
            for fc in range(128):
                h_, gp_ = fc // 32, fc % 32
                hr = h_ * 64 + 2 * gp_
                slot = fc % 4
                w = [("dw", n_dw1), ("dv", n_zero)] if fc == 0 else []
                if fc >= 4:
                    w = [("dv", n_zero + (fc - 4) + 1)]
                T_((lambda t, hr=hr, slot=slot: t.matmul(
                    psB[0:32, slot * 512:slot * 512 + 128],
                    rmt_bf[:, hr * 16:hr * 16 + 32],
                    rut_bf[:, hr * 64:hr * 64 + 128],
                    start=True, stop=True)), waits=w)
                q4, rg = fc // 4, fc % 4
                V_((lambda v, slot=slot, q4=q4, rg=rg: v.tensor_tensor(
                    railsM[rg * 32:(rg + 1) * 32, q4 * 128:(q4 + 1) * 128],
                    psB[0:32, slot * 512:slot * 512 + 128],
                    diagmask_bf[:, :], op=ALU.mult)),
                   waits=[("pe", P.snap("pe"))])
        rails_built_pe = P.snap("pe")
        rails_built_dv = P.snap("dv")
        ln3 = sb("ln3", (128, DB * 1024), BF16)
        wc0 = sb("wc0", (128, 8 * 1024), BF16)
        wc1 = sb("wc1", (128, 8 * 1024), BF16)
        hsl0 = sb("hsl0", (128, 16 * 512), BF16)
        hsl1 = sb("hsl1", (128, 16 * 512), BF16)
        hslb = [hsl0, hsl1]
        oacc = sb("oacc", (128, 8 * 1024), F32)
        wcbufs = [wc0, wc1]
        wc_free_dv = [rails_built_dv, rails_built_dv]
        hsl_free = [rails_built_pe, rails_built_pe]

        def emit_wc_load(si):
            buf = wcbufs[si % 2]
            GP((lambda g, si=si, buf=buf: g.dma_start(
                AP(buf, 0, [[8 * 1024, 128], [1024, 8], [1, 1024]]),
                AP(wcT, si * 1024 * 1024, [[1024, 128], [128 * 1024, 8], [1, 1024]]))),
               waits=[("dv", wc_free_dv[si % 2])], inc="dwc")
            return P.snap("dwc")

        for half in range(2):
            t0 = half * 1024
            for cc in range(4):
                V_((lambda v, cc=cc, t0=t0: v.tensor_copy(
                    ap3(xa_bf),
                    AP(x_sb, t0 + cc * 256, [[DB * NOWN, 128], [NOWN, 8], [1, 256]]))),
                   waits=[("dv", attn_done_dv), ("pe", rails_built_pe)]
                   if (half == 0 and cc == 0) else [])
                emit_ln(xa_bf, pn_bf, psA, sq_bf, f32s, mr_bf)
                V_((lambda v, cc=cc: v.tensor_copy(
                    AP(ln3, cc * 256, [[DB * 1024, 128], [1024, 8], [1, 256]]),
                    ap3(pn_bf))),
                   waits=[("pe", rails_built_pe)] if (half == 0 and cc == 0) else [])
            ln3_done = P.snap("dv")
            if FFN_UPTO < 2:
                continue

            wc_sem = [emit_wc_load(0), emit_wc_load(1)]
            evac_dv = [P.snap("dv")] * 4
            seqs = [(grp, g) for grp in range(8) for g in range(2)]
            h_done_for = {}
            gelu_ac = []

            def emit_rail_one(i, fcl, first_w=()):
                grp, g = seqs[i]
                fc = grp * 16 + fcl
                gp_ = fc % 32
                q4, rg = fc // 4, fc % 4
                dbl, pbase = gp_ // 4, (gp_ % 4) * 32
                zsl = (i * 16 + fcl) % 2
                hb = hslb[i % 2]
                w = list(first_w)
                zi = i * 16 + fcl
                if zi >= 2:
                    w.append(("ac", gelu_ac[zi - 2]))
                T_((lambda t, q4=q4, rg=rg, dbl=dbl, pbase=pbase, g=g,
                    zsl=zsl: t.matmul(
                        psA[:, zsl * 512:(zsl + 1) * 512],
                        railsM[rg * 32:(rg + 1) * 32, q4 * 128:(q4 + 1) * 128],
                        ln3[pbase:pbase + 32,
                            dbl * 1024 + g * 512:dbl * 1024 + (g + 1) * 512],
                        start=True, stop=True,
                        tile_position=(rg * 32, 0))), waits=w)
                pz = P.snap("pe")
                wg = [("pe", pz)]
                if fcl == 0:
                    wg.append(("pe", hsl_free[i % 2]))
                A_((lambda a, fcl=fcl, zsl=zsl, hb=hb: a.activation(
                    hb[:, fcl * 512:(fcl + 1) * 512],
                    psA[:, zsl * 512:(zsl + 1) * 512], AF.Gelu)), waits=wg)
                gelu_ac.append(P.snap("ac"))
                if fcl == 15:
                    h_done_for[i] = P.snap("ac")

            fw0 = [("dv", ln3_done)]
            for fcl in range(16):
                emit_rail_one(0, fcl, fw0 if fcl == 0 else ())
            if FFN_UPTO >= 3:
                for i, (grp, g) in enumerate(seqs):
                    hb = hslb[i % 2]
                    for obg in range(2):
                        dv_guard = list(evac_dv)
                        for fcl in range(16):
                            if obg == 0 and i + 1 < len(seqs):
                                emit_rail_one(i + 1, fcl)
                            for ob in range(4):
                                obi = obg * 4 + ob
                                w = []
                                if fcl == 0:
                                    w = [("dv", dv_guard[ob])]
                                    if ob == 0:
                                        w += [("ac", h_done_for[i]),
                                              ("dwc", max(wc_sem))]
                                T_((lambda t, fcl=fcl, ob=ob, obi=obi, hb=hb:
                                    t.matmul(
                                        psB[:, ob * 512:(ob + 1) * 512],
                                        wcbufs[fcl // 8][:,
                                            (fcl % 8) * 1024 + obi * 128:
                                            (fcl % 8) * 1024 + (obi + 1) * 128],
                                        hb[:, fcl * 512:(fcl + 1) * 512],
                                        start=(fcl == 0), stop=(fcl == 15))),
                                   waits=w)
                        pm = P.snap("pe")
                        if obg == 1:
                            hsl_free[i % 2] = pm
                        for ob in range(4):
                            if grp == 0:
                                V_((lambda v, g=g, obg=obg, ob=ob: v.tensor_copy(
                                    AP(oacc, (obg * 4 + ob) * 1024 + g * 512,
                                       [[8 * 1024, 128], [1, 512]]),
                                    psB[:, ob * 512:(ob + 1) * 512])),
                                   waits=[("pe", pm), ("dout", P.snap("dout"))]
                                   if ob == 0 else [("pe", pm)])
                            else:
                                V_((lambda v, g=g, obg=obg, ob=ob: v.tensor_tensor(
                                    AP(oacc, (obg * 4 + ob) * 1024 + g * 512,
                                       [[8 * 1024, 128], [1, 512]]),
                                    AP(oacc, (obg * 4 + ob) * 1024 + g * 512,
                                       [[8 * 1024, 128], [1, 512]]),
                                    psB[:, ob * 512:(ob + 1) * 512],
                                    op=ALU.add)), waits=[("pe", pm)])
                            evac_dv[ob] = P.snap("dv")
                    if g == 1:
                        wc_free_dv[0] = P.snap("dv")
                        wc_free_dv[1] = P.snap("dv")
                        if grp < 7:
                            wc_sem = [emit_wc_load(2 * (grp + 1)),
                                      emit_wc_load(2 * (grp + 1) + 1)]
            y_done = P.snap("dv")
            if FFN_UPTO < 4:
                continue

            for cc in range(4):
                V_((lambda v, cc=cc: v.tensor_copy(
                    ap3(xa_bf),
                    AP(oacc, cc * 256, [[8 * 1024, 128], [1024, 8], [1, 256]]))),
                   waits=[("dv", y_done)] if cc == 0 else [])
                emit_ln(xa_bf, pn_bf, psA, sq_bf, f32s, mr_bf)
                if FFN_UPTO < 5:
                    continue
                V_((lambda v, cc=cc, t0=t0: v.tensor_tensor(
                    AP(oacc, cc * 256, [[8 * 1024, 128], [1024, 8], [1, 256]]),
                    AP(x_sb, t0 + cc * 256, [[DB * NOWN, 128], [NOWN, 8], [1, 256]]),
                    ap3(pn_bf), op=ALU.add)))
                dfin = P.snap("dv")
                if FFN_UPTO < 6:
                    continue
                SY((lambda s, cc=cc, t0=t0: s.dma_start(
                    AP(y_out, t0 + cc * 256,
                       [[NOWN, 128], [NOWN * 128, 8], [1, 256]]),
                    AP(oacc, cc * 256, [[8 * 1024, 128], [1024, 8], [1, 256]]))),
                   waits=[("dv", dfin)], inc="dout")

        n_out = P.snap("dout")
        SY(lambda s: s.nop(), waits=[("dout", n_out)])
        GP(lambda g: g.nop(), waits=[("ddbg", P.snap("ddbg"))])

        P.run(nc, sems)

    return nc, dbg


_NC_CACHE = {}


def _get_nc(debug=()):
    key = tuple(sorted(debug))
    if key not in _NC_CACHE:
        _NC_CACHE[key] = build_nc(debug)
    return _NC_CACHE[key]


def stage_inputs(L0_attn_x, L0_attn_w_qkv, L0_attn_b_qkv, L0_attn_w_out,
                 L0_attn_b_out, L0_attn_omega, L0_attn_quad_nodes,
                 L0_attn_quad_weights, L0_ffn_rail_maker, L0_ffn_rail_up_proj,
                 L0_ffn_w_cproj):
    x = np.ascontiguousarray(np.asarray(L0_attn_x, dtype=np.float32))
    wqkvT = np.ascontiguousarray(np.asarray(L0_attn_w_qkv, np.float32).T)
    bqkv = np.asarray(L0_attn_b_qkv, np.float32)
    bqkv_c = np.ascontiguousarray(bqkv[:2048].reshape(16, 128).T)
    bqv_row = np.ascontiguousarray(bqkv[2048:].reshape(1, DM))
    woutT = np.ascontiguousarray(np.asarray(L0_attn_w_out, np.float32).T)
    bout_c = np.ascontiguousarray(
        np.asarray(L0_attn_b_out, np.float32).reshape(8, 128).T)
    omega2 = np.ascontiguousarray(
        np.asarray(L0_attn_omega, np.float32).transpose(2, 1, 0, 3).reshape(64, 1024))
    qn = np.asarray(L0_attn_quad_nodes, np.float32)
    qw = np.asarray(L0_attn_quad_weights, np.float32)
    ridx = (np.arange(128) // 32) % 2
    qnw_col = np.ascontiguousarray(np.stack([qn[ridx], qw[ridx]], axis=1))
    rmt = np.ascontiguousarray(
        np.asarray(L0_ffn_rail_maker, np.float32).transpose(3, 0, 1, 2).reshape(16, 4096))
    rut = np.ascontiguousarray(
        np.asarray(L0_ffn_rail_up_proj, np.float32).transpose(3, 0, 1, 2).reshape(16, 16384))
    wcT = np.ascontiguousarray(np.asarray(L0_ffn_w_cproj, np.float32).T)

    c_eye = np.eye(128, dtype=np.float32)
    c_tri = np.triu(np.ones((128, 128), np.float32))
    c_allones = np.ones((128, 128), np.float32)
    bo = np.zeros((128, 128), np.float32)
    bo[:64, :64] = 1.0
    bo[64:, 64:] = 1.0
    dmsk = np.zeros((32, 128), np.float32)
    dmsk[:16, :64] = 1.0
    dmsk[16:, 64:] = 1.0
    m384 = np.ones((128, 384), np.float32)
    m384[:, 0:128] = np.triu(np.ones((128, 128), np.float32))
    m384[:, 256:384] = np.triu(np.ones((128, 128), np.float32))

    shared = dict(wqkvT=wqkvT, bqkv_c=bqkv_c, bqv_row=bqv_row, woutT=woutT,
                  bout_c=bout_c, omega2=omega2, qnw_col=qnw_col, rmt=rmt, rut=rut,
                  wcT=wcT, c_eye=c_eye, c_tri=c_tri, c_allones=c_allones,
                  c_blockones=bo, c_diagmask=dmsk, c_mask384=m384)

    in_maps = []
    for core in range(8):
        b, half = core // 2, core % 2
        xb = np.ascontiguousarray(x[b].T)
        m = dict(shared)
        m["x_own"] = np.ascontiguousarray(xb[:, half * NOWN:(half + 1) * NOWN])
        if half == 1:
            m["x_carry"] = np.ascontiguousarray(xb[:, 0:NOWN])
            m["cscale"] = np.ones((128, 1), np.float32)
        else:
            m["x_carry"] = np.zeros((DM, NOWN), np.float32)
            m["cscale"] = np.zeros((128, 1), np.float32)
        in_maps.append(m)
    return in_maps


def kernel(L0_attn_x, L0_attn_w_qkv, L0_attn_b_qkv, L0_attn_w_out, L0_attn_b_out,
           L0_attn_omega, L0_attn_quad_nodes, L0_attn_quad_weights,
           L0_ffn_rail_maker, L0_ffn_rail_up_proj, L0_ffn_w_cproj, _debug=()):
    in_maps = stage_inputs(L0_attn_x, L0_attn_w_qkv, L0_attn_b_qkv, L0_attn_w_out,
                           L0_attn_b_out, L0_attn_omega, L0_attn_quad_nodes,
                           L0_attn_quad_weights, L0_ffn_rail_maker,
                           L0_ffn_rail_up_proj, L0_ffn_w_cproj)
    nc, dbg = _get_nc(_debug)
    res = run_bass_kernel_spmd(nc, in_maps, core_ids=list(range(8)))
    out = np.empty((B, T, DM), np.float32)
    for core in range(8):
        b, half = core // 2, core % 2
        out[b, half * NOWN:(half + 1) * NOWN, :] = res.results[core]["y_out"].T
    if _debug:
        return out, res
    return out


# version 13 (final)
# speedup vs baseline: 1.0133x; 1.0133x over previous
import numpy as np
from contextlib import ExitStack

import concourse.bass as bass
import concourse.mybir as mybir
from concourse.bass_utils import run_bass_kernel_spmd

F32 = mybir.dt.float32
BF16 = mybir.dt.bfloat16
AF = mybir.ActivationFunctionType
ALU = mybir.AluOpType
AP = bass.AP

B, T, DM = 4, 4096, 1024
H = 16
NOWN = 2048
CH = 256
NCH = NOWN // CH
DB = 8
EPS = 1e-5
E20 = float(np.exp(np.float32(20.0)))
EM20 = float(np.exp(np.float32(-20.0)))
INV_SQRT32 = float(np.float32(1.0) / np.float32(np.sqrt(np.float32(32.0))))

ENGINES = ("tensor", "vector", "scalar", "sync", "gpsimd")
DMA_SEMS = ("din", "dw", "dxa", "dwc", "dout", "ddbg")
CHUNK_UPTO = 99
KFT_EVAC = 1
FFN_UPTO = 99


class Prog:

    def __init__(self, semnames):
        self.q = {e: [] for e in ENGINES}
        self.cnt = {s: 0 for s in semnames}
        self.sems = None
        self._seq = []
        self._sno = 0

    def emit(self, eng, fn, waits=(), inc=None, inc_n=1):
        self._seq.append((self._sno, eng, len(self.q[eng])))
        self._sno += 1
        self.q[eng].append((tuple(waits), fn, inc, inc_n))
        if inc is not None:
            self.cnt[inc] += inc_n

    def snap(self, sem):
        return self.cnt[sem]

    def run(self, nc, sems):
        self.sems = sems
        prog = self
        waited = {s: set() for s in self.cnt}
        for elist in self.q.values():
            for waits, fn, inc, inc_n in elist:
                for s, v in waits:
                    if v > 0:
                        waited[s].add(v)
        rank = {}
        for s, vals in waited.items():
            rank[s] = {}
            for i, v in enumerate(sorted(vals)):
                rank[s][v] = i + 1
        raw = {s: 0 for s in self.cnt}
        seq = sorted(self._seq, key=lambda t: t[0])
        signal_for = {}
        for sno, e, idx in seq:
            waits, fn, inc, inc_n = self.q[e][idx]
            if inc is not None:
                raw[inc] += inc_n
                signal_for[(e, idx)] = True
        with nc.Block() as blk:
            def body(engobj, ename, elist):
                deep = engobj is nc.vector or engobj is nc.scalar
                for idx, (waits, fn, inc, inc_n) in enumerate(elist):
                    for s, v in waits:
                        if v > 0:
                            engobj.wait_ge(prog.sems[s], v)
                    ins = fn(engobj)
                    if inc is not None and signal_for.get((ename, idx), False):
                        if deep:
                            engobj.drain(fusable=True).then_inc(
                                prog.sems[inc], inc_n)
                        else:
                            ins.then_inc(prog.sems[inc], inc_n)
            blk.tensor(lambda e: body(e, "tensor", prog.q["tensor"]))
            blk.vector(lambda e: body(e, "vector", prog.q["vector"]))
            blk.scalar(lambda e: body(e, "scalar", prog.q["scalar"]))
            blk.sync(lambda e: body(e, "sync", prog.q["sync"]))
            blk.gpsimd(lambda e: body(e, "gpsimd", prog.q["gpsimd"]))


def build_nc(debug=(), n_carry=NCH, n_main=NCH, do_ffn=True, dumps=()):
    nc = bass.Bass("TRN2", target_bir_lowering=False)
    dbg = {}

    din = {}
    def dram_in(name, shape):
        din[name] = nc.dram_tensor(name, list(shape), F32, kind="ExternalInput")
        return din[name]

    x_own = dram_in("x_own", (DM, NOWN))
    x_carry = dram_in("x_carry", (DM, NOWN))
    cscale = dram_in("cscale", (128, 1))
    wqkvT = dram_in("wqkvT", (DM, 3 * DM))
    bqkv_c = dram_in("bqkv_c", (128, 16))
    bqv_row = dram_in("bqv_row", (1, DM))
    woutT = dram_in("woutT", (DM, DM))
    bout_c = dram_in("bout_c", (128, 8))
    omega2 = dram_in("omega2", (64, 1024))
    qnw_col = dram_in("qnw_col", (128, 2))
    rmt = dram_in("rmt", (16, 4096))
    rut = dram_in("rut", (16, 16384))
    wcT = dram_in("wcT", (16384, DM))
    c_eye = dram_in("c_eye", (128, 128))
    c_tri = dram_in("c_tri", (128, 128))
    c_allones = dram_in("c_allones", (128, 128))
    c_blockones = dram_in("c_blockones", (128, 128))
    c_diagmask = dram_in("c_diagmask", (32, 128))
    c_mask384 = dram_in("c_mask384", (128, 384))

    y_out = nc.dram_tensor("y_out", [DM, NOWN], F32, kind="ExternalOutput")
    for name, shape in debug:
        dbg[name] = nc.dram_tensor(name, list(shape), F32, kind="ExternalOutput")

    es = ExitStack()
    with es:
        sb = lambda n, shp, dt: es.enter_context(nc.sbuf_tensor(n, list(shp), dt))
        x_sb = sb("x_sb", (128, DB * NOWN), F32)
        railsM = sb("railsM", (128, 32 * 128), BF16)
        mask384_bf = railsM
        omega_bf = sb("omega_bf", (128, 1024), BF16)
        eye_bf = sb("eye_bf", (128, 128), BF16)
        allones_bf = sb("allones_bf", (128, 128), BF16)
        blockones_bf = sb("blockones_bf", (128, 128), BF16)
        diagmask_bf = sb("diagmask_bf", (32, 128), BF16)
        bqkv_sb = sb("bqkv_sb", (128, 16), F32)
        bout_sb = sb("bout_sb", (128, 8), F32)
        svecs = sb("svecs", (128, 4), F32)
        cscale_sb = sb("cscale_sb", (128, 1), F32)
        qnw_sb = sb("qnw_sb", (128, 2), F32)
        small_f = sb("small_f", (128, 4), F32)
        eps_c = sb("eps_c", (128, 1), F32)
        nc.const_aps.register(F32, EPS, eps_c[:, :])

        sems = {}
        for s in ("pe", "dv", "ac", "din", "dw", "dxa", "dwc", "dout", "ddbg"):
            sems[s] = es.enter_context(nc.semaphore("sem_" + s))

        P = Prog(list(sems))

        def T_(fn, waits=()):
            P.emit("tensor", fn, waits, "pe")

        def V_(fn, waits=()):
            P.emit("vector", fn, waits, "dv")

        def A_(fn, waits=()):
            P.emit("scalar", fn, waits, "ac")

        def SY(fn, waits=(), inc=None):
            P.emit("sync", fn, waits, inc, 16)

        def GP(fn, waits=(), inc=None):
            P.emit("gpsimd", fn, waits, inc, 16)

        def ap3(t, off=0, w=None, blocks=DB, cw=256):
            W = t.shape[1]
            return AP(t, off, [[W, 128], [cw if w is None else w, blocks], [1, cw]])

        for sbuf, dr in ((allones_bf, c_allones),
                         (blockones_bf, c_blockones), (eye_bf, c_eye),
                         (diagmask_bf, c_diagmask)):
            GP((lambda g, a=sbuf, b=dr: g.dma_start(a[:, :], b[:, :])), inc="dw")
        GP(lambda g: g.dma_start(mask384_bf[:, 0:384], c_mask384[:, :]), inc="dw")
        GP(lambda g: g.dma_start(railsM[0:1, 384:1408], bqv_row[:, :]), inc="dw")
        GP(lambda g: g.dma_start(omega_bf[0:64, :], omega2[:, :]), inc="dw")
        GP(lambda g: g.dma_start(omega_bf[64:128, :], omega2[:, :]), inc="dw")
        for sbuf, dr in ((bqkv_sb, bqkv_c),
                         (bout_sb, bout_c), (qnw_sb, qnw_col),
                         (cscale_sb, cscale)):
            SY((lambda s, a=sbuf, b=dr: s.dma_start(a[:, :], b[:, :])), inc="din")
        for db in range(DB):
            SY((lambda s, db=db: s.dma_start(
                AP(x_sb, db * NOWN, [[DB * NOWN, 128], [1, NOWN]]),
                AP(x_own, db * 128 * NOWN, [[NOWN, 128], [1, NOWN]]))), inc="din")
        n_din0 = P.snap("din")

        psA = es.enter_context(nc.psum_tensor("psA", [128, 1024], F32))
        psB = es.enter_context(nc.psum_tensor("psB", [128, 2048], F32))
        psT = es.enter_context(nc.psum_tensor("psT", [128, 2048], BF16))

        V_(lambda v: v.memset(railsM[0:1, 1408:1536], 1.0))
        V_(lambda v: v.memset(eps_c[:, :], EPS))
        V_(lambda v: v.tensor_scalar(small_f[:, 0:1], qnw_sb[:, 0:1], 0.0, None,
                                     op0=ALU.max), waits=[("din", n_din0)])
        V_(lambda v: v.tensor_scalar(small_f[:, 1:2], small_f[:, 0:1], 2.0, 1e-30,
                                     op0=ALU.mult, op1=ALU.max))
        d_sv = P.snap("dv")
        A_(lambda a: a.activation(small_f[:, 2:3], small_f[:, 1:2], AF.Ln),
           waits=[("dv", d_sv)])
        A_(lambda a: a.activation(svecs[:, 0:1], small_f[:, 2:3], AF.Exp, scale=0.5))
        V_(lambda v: v.tensor_scalar(svecs[:, 1:2], qnw_sb[:, 0:1], -1.0, None,
                                     op0=ALU.mult))
        V_(lambda v: v.tensor_scalar(small_f[:, 3:4], qnw_sb[:, 1:2], 0.0, 1e-30,
                                     op0=ALU.max, op1=ALU.max))
        d_sv2 = P.snap("dv")
        A_(lambda a: a.activation(small_f[:, 0:1], small_f[:, 3:4], AF.Ln),
           waits=[("dv", d_sv2)])
        A_(lambda a: a.activation(small_f[:, 1:2], small_f[:, 0:1], AF.Exp,
                                  scale=0.25))
        V_(lambda v: v.tensor_scalar(svecs[:, 2:3], small_f[:, 1:2], INV_SQRT32,
                                     None, op0=ALU.mult),
           waits=[("ac", P.snap("ac"))])

        xa_bf = sb("xa_bf", (128, 2048), BF16)
        sq_bf = sb("sq_bf", (128, 2048), BF16)
        pn_bf = sb("pn_bf", (128, 2048), BF16)
        f32s = sb("f32s", (128, 1024), F32)
        mr_bf = sb("mr_bf", (128, 512), BF16)
        es_attn = ExitStack()
        sba = lambda n, shp, dt: es_attn.enter_context(nc.sbuf_tensor(n, list(shp), dt))
        wqkv_bf = sba("wqkv_bf", (128, DB * 3072), BF16)
        wout_bf = sba("wout_bf", (128, DB * 1024), BF16)
        for db in range(DB):
            GP((lambda g, db=db: g.dma_start(
                AP(wqkv_bf, db * 3072, [[DB * 3072, 128], [1, 3072]]),
                AP(wqkvT, db * 128 * 3072, [[3072, 128], [1, 3072]]))), inc="dw")
        for db in range(DB):
            GP((lambda g, db=db: g.dma_start(
                AP(wout_bf, db * 1024, [[DB * 1024, 128], [1, 1024]]),
                AP(woutT, db * 128 * 1024, [[1024, 128], [1, 1024]]))), inc="dw")
        n_dw0 = P.snap("dw")
        ln1_bf = sba("ln1_bf", (128, 2048), BF16)
        qk_bf = sba("qk_bf", (128, 2048), BF16)
        qf2 = [sba("qf_a", (128, 2048), BF16), sba("qf_b", (128, 2048), BF16)]
        kf2 = [sba("kf_a", (128, 2048), BF16), sba("kf_b", (128, 2048), BF16)]
        kT2 = [sba("kT_a", (128, 2048), BF16), sba("kT_b", (128, 2048), BF16)]
        va2 = [sba("va_a", (128, 2 * 1056), BF16), sba("va_b", (128, 2 * 1056), BF16)]
        ctx_bf = sba("ctx_bf", (128, 2048), BF16)
        ab_bf = railsM[:, 3424:3808]
        S_f = sba("S_f", (128, 8 * 66), F32)
        S_bf = railsM[:, 1856:2384]
        denrep = railsM[:, 2384:2896]
        dinv = railsM[:, 1536:1792]
        f32sB = sba("f32sB", (128, 1024), F32)
        mrB = sba("mrB", (128, 512), BF16)

        for par in range(2):
            for sub in range(2):
                V_((lambda v, par=par, sub=sub: v.memset(
                    AP(va2[par], sub * 1056 + 64,
                       [[2 * 1056, 128], [66, 16], [1, 1]]), 1.0)))
        V_(lambda v: v.memset(S_f[:, :], 0.0))
        V_(lambda v: v.memset(S_bf[:, :], 0.0))

        def emit_ln(xb, out_buf, ps, scr_sq, scr_f, scr_mr, first_waits=(),
                    resid_cols=None):
            d0 = P.snap("dv")
            pe0 = P.snap("pe")
            A_((lambda a: a.activation(scr_sq[:, :], xb[:, :], AF.Square)),
               waits=list(first_waits) + [("dv", d0), ("pe", pe0)])
            a0 = P.snap("ac")
            for db in range(DB):
                T_((lambda t, db=db: t.matmul(
                    ps[:, 0:256], allones_bf[:, :],
                    xb[:, db * 256:(db + 1) * 256],
                    start=(db == 0), stop=(db == DB - 1))),
                   waits=(list(first_waits) + [("dv", d0)]) if db == 0 else [])
            for db in range(DB):
                T_((lambda t, db=db: t.matmul(
                    ps[:, 256:512], allones_bf[:, :],
                    scr_sq[:, db * 256:(db + 1) * 256],
                    start=(db == 0), stop=(db == DB - 1))),
                   waits=[("ac", a0)] if db == 0 else [])
            p0 = P.snap("pe")
            V_((lambda v: v.tensor_scalar(scr_f[:, 0:256], ps[:, 0:256], 1.0 / DM,
                                          None, op0=ALU.mult)), waits=[("pe", p0)])
            V_((lambda v: v.tensor_tensor(scr_f[:, 256:512], scr_f[:, 0:256],
                                          scr_f[:, 0:256], op=ALU.mult)))
            V_((lambda v: v.scalar_tensor_tensor(scr_f[:, 512:768], ps[:, 256:512],
                                                 1.0 / DM, scr_f[:, 256:512],
                                                 op0=ALU.mult, op1=ALU.subtract)))
            V_((lambda v: v.tensor_copy(scr_mr[:, 0:256], scr_f[:, 0:256])))
            d1 = P.snap("dv")
            A_((lambda a: a.activation(scr_f[:, 768:1024], scr_f[:, 512:768], AF.Ln,
                                       bias=EPS)), waits=[("dv", d1)])
            A_((lambda a: a.activation(scr_mr[:, 256:512], scr_f[:, 768:1024],
                                       AF.Exp, scale=-0.5)))
            a1 = P.snap("ac")
            rep_m = AP(scr_mr, 0, [[512, 128], [0, 8], [1, 256]])
            rep_r = AP(scr_mr, 256, [[512, 128], [0, 8], [1, 256]])
            xb3 = ap3(xb)
            ob3 = ap3(out_buf)
            V_((lambda v: v.tensor_tensor(ob3, xb3, rep_m, op=ALU.subtract)),
               waits=[("ac", a1)])
            V_((lambda v: v.tensor_tensor(ob3, ob3, rep_r, op=ALU.mult)))
            if resid_cols is not None:
                xap = AP(x_sb, resid_cols, [[DB * NOWN, 128], [NOWN, 8], [1, 256]])
                V_((lambda v: v.tensor_tensor(xap, xap, ob3, op=ALU.add)))

        def emit_features(src_bf, out_bf, is_carry):
            d0 = P.snap("dv")
            pe0 = P.snap("pe")
            A_((lambda a: a.activation(sq_bf[:, :], src_bf[:, :], AF.Square)),
               waits=[("dv", d0), ("pe", pe0)])
            a0 = P.snap("ac")
            for half in range(2):
                c0 = half * 1024
                dvp = P.snap("dv")
                for bb4 in range(4):
                    bb = half * 4 + bb4
                    T_((lambda t, bb=bb, bb4=bb4: t.matmul(
                        psA[:, bb4 * 256:(bb4 + 1) * 256], blockones_bf[:, :],
                        sq_bf[:, bb * 256:(bb + 1) * 256], start=True, stop=True)),
                       waits=[("ac", a0), ("dv", dvp)] if bb4 == 0 else [])
                p0 = P.snap("pe")
                V_((lambda v: v.tensor_scalar(
                    f32s[:, 0:1024], psA[:, 0:1024], 1e-24,
                    None, op0=ALU.max)), waits=[("pe", p0)])
                d1 = P.snap("dv")
                A_((lambda a: a.activation(f32s[:, 0:1024], f32s[:, 0:1024],
                                           AF.Ln)), waits=[("dv", d1)])
                A_((lambda a, c0=c0: a.activation(
                    xa_bf[:, c0:c0 + 1024], f32s[:, 0:1024], AF.Exp, scale=-0.5)))
                a1 = P.snap("ac")
                V_((lambda v, c0=c0: v.tensor_scalar(
                    xa_bf[:, c0:c0 + 1024], xa_bf[:, c0:c0 + 1024], 1e12, None,
                    op0=ALU.min)), waits=[("ac", a1)])
                inv_done = P.snap("dv")
                for bb4 in range(4):
                    bb = half * 4 + bb4
                    for hh in range(2):
                        hcur = 2 * bb + hh
                        lo = 64 * hh
                        kw = {} if hh == 0 else {"tile_position": (64, 64)}
                        T_((lambda t, bb=bb, bb4=bb4, lo=lo, hcur=hcur, kw=kw:
                            t.matmul(
                                psA[lo:lo + 64, bb4 * 256:(bb4 + 1) * 256],
                                omega_bf[lo:lo + 64, hcur * 64:(hcur + 1) * 64],
                                src_bf[lo:lo + 64, bb * 256:(bb + 1) * 256],
                                start=True, stop=True, **kw)),
                           waits=[("dv", inv_done)] if (bb4 == 0 and hh == 0) else [])
                p1 = P.snap("pe")
                V_((lambda v, c0=c0: v.tensor_tensor(
                    pn_bf[:, c0:c0 + 1024], psA[:, 0:1024],
                    xa_bf[:, c0:c0 + 1024], op=ALU.mult)), waits=[("pe", p1)])
            ret = P.snap("dv")
            d2 = P.snap("dv")
            pe2 = P.snap("pe")
            A_((lambda a: a.activation(sq_bf[:, :], pn_bf[:, :], AF.Exp,
                                       bias=svecs[:, 1:2], scale=svecs[:, 0:1])),
               waits=[("dv", d2), ("pe", pe2)])
            A_((lambda a: a.activation(xa_bf[:, :], pn_bf[:, :], AF.Square,
                                       scale=svecs[:, 2:3])))
            a2 = P.snap("ac")
            V_((lambda v: v.tensor_scalar(pn_bf[:, :], sq_bf[:, :], E20, EM20,
                                          op0=ALU.min, op1=ALU.max)),
               waits=[("ac", a2)])
            if is_carry:
                V_((lambda v: v.scalar_tensor_tensor(out_bf[:, :], xa_bf[:, :],
                                                     cscale_sb[:, :], pn_bf[:, :],
                                                     op0=ALU.mult, op1=ALU.mult)))
            else:
                V_((lambda v: v.tensor_tensor(out_bf[:, :], xa_bf[:, :],
                                              pn_bf[:, :], op=ALU.mult)))
            return ret

        marks = {}

        def stageA_gen(ci, carry, parity):
            qfb, kfb, kTb, vab = qf2[parity], kf2[parity], kT2[parity], va2[parity]
            if carry:
                GP((lambda g, ci=ci: g.dma_start(
                    AP(xa_bf, 0, [[2048, 128], [256, 8], [1, 256]]),
                    AP(x_carry, ci * 256, [[NOWN, 128], [128 * NOWN, 8], [1, 256]]))),
                   waits=[("dv", P.snap("dv"))], inc="dxa")
                fw = [("dxa", P.snap("dxa"))]
            else:
                A_((lambda a, ci=ci: a.activation(
                    ap3(xa_bf),
                    AP(x_sb, ci * 256, [[DB * NOWN, 128], [NOWN, 8], [1, 256]]),
                    AF.Copy)), waits=[("din", n_din0)])
                fw = [("ac", P.snap("ac"))]
            fw = fw + [("dw", n_dw0)]
            yield
            emit_ln(xa_bf, ln1_bf, psA, sq_bf, f32s, mr_bf, first_waits=fw)
            ln_done = P.snap("dv")
            yield

            def qkv_pass(jb0, dst, dstoff, first_w):
                for jj in range(4):
                    jb = jb0 + jj
                    for db in range(DB):
                        T_((lambda t, jj=jj, jb=jb, db=db: t.matmul(
                            psA[:, jj * 256:(jj + 1) * 256],
                            wqkv_bf[:, db * 3072 + jb * 128:db * 3072 + (jb + 1) * 128],
                            ln1_bf[:, db * 256:(db + 1) * 256],
                            start=(db == 0), stop=(db == DB - 1))),
                           waits=list(first_w) + [("dw", n_dw0)]
                           if (jj == 0 and db == 0) else [])
                pp = P.snap("pe")
                for jj in range(4):
                    A_((lambda a, dst=dst, dstoff=dstoff, jj=jj, jb0=jb0:
                        a.activation(
                            dst[:, dstoff + jj * 256:dstoff + (jj + 1) * 256],
                            psA[:, jj * 256:(jj + 1) * 256], AF.Identity,
                            bias=bqkv_sb[:, jb0 + jj:jb0 + jj + 1])),
                       waits=[("pe", pp)] if jj == 0 else [])
                return P.snap("ac")

            free = [("dv", ln_done)]
            if not carry:
                for pp_ in range(2):
                    free = [("ac", qkv_pass(pp_ * 4, qk_bf, pp_ * 1024, free))]
                    yield
                free = [("dv", emit_features(qk_bf, qfb, False))]
                yield
            for pp_ in range(2):
                free = [("ac", qkv_pass(8 + pp_ * 4, qk_bf, pp_ * 1024, free))]
                yield
            free = [("dv", emit_features(qk_bf, kfb, carry))]
            kf_done = P.snap("dv")
            yield
            for sub in range(2):
                for db in range(DB):
                    for oh in range(2):
                        T_((lambda t, sub=sub, db=db, oh=oh: t.matmul(
                            psA[:, oh * 512:(oh + 1) * 512],
                            ln1_bf[:, db * 256 + sub * 128:db * 256 + sub * 128 + 128],
                            wqkv_bf[:, db * 3072 + 2048 + oh * 512:
                                    db * 3072 + 2048 + (oh + 1) * 512],
                            start=(db == 0), stop=False)),
                           waits=list(free) if (db == 0 and oh == 0) else [])
                for oh in range(2):
                    T_((lambda t, oh=oh: t.matmul(
                        psA[:, oh * 512:(oh + 1) * 512],
                        railsM[0:1, 1408:1536],
                        railsM[0:1, 384 + oh * 512:384 + (oh + 1) * 512],
                        start=False, stop=True)))
                pv = P.snap("pe")
                A_((lambda a, sub=sub, vab=vab: a.activation(
                    AP(vab, sub * 1056, [[2 * 1056, 128], [66, 16], [1, 64]]),
                    AP(psA, 0, [[1024, 128], [64, 16], [1, 64]]), AF.Copy)),
                   waits=[("pe", pv)])
                free = [("ac", P.snap("ac"))]
                yield
            def ftr(t, kfb=kfb):
                last = None
                for i in range(16):
                    if i:
                        t.nop()
                    last = t.transpose(psT[:, i * 128:(i + 1) * 128],
                                       kfb[:, i * 128:(i + 1) * 128],
                                       eye_bf[:, :])
                return last
            T_(ftr, waits=[("dv", kf_done)])
            pt = P.snap("pe")
            A_((lambda a, kTb=kTb: a.activation(kTb[:, :], psT[:, :], AF.Copy)),
               waits=[("pe", pt)])
            marks[ci] = dict(kfT_done_ac=P.snap("ac"))
            yield
            if carry:
                for hh in range(H):
                    lo = (hh % 2) * 64
                    bb = hh // 2
                    off = hh * 128
                    kw = {} if hh % 2 == 0 else {"tile_position": (0, 64)}
                    for sub in range(2):
                        T_((lambda t, hh=hh, lo=lo, bb=bb, off=off, sub=sub,
                            kw=kw, kTb=kTb, vab=vab: t.matmul(
                                psB[lo:lo + 64, off:off + 65],
                                kTb[:, (bb * 2 + sub) * 128 + lo:
                                    (bb * 2 + sub) * 128 + lo + 64],
                                vab[:, sub * 1056 + 66 * hh:sub * 1056 + 66 * hh + 65],
                                start=(sub == 0), stop=(sub == 1), **kw)),
                           waits=[("ac", marks[ci]["kfT_done_ac"])]
                           if (hh == 0 and sub == 0) else [])
                pS = P.snap("pe")
                for hh in range(H):
                    lo = (hh % 2) * 64
                    off = hh * 128
                    V_((lambda v, hh=hh, lo=lo, off=off: v.tensor_tensor(
                        S_f[lo:lo + 64, 66 * (hh // 2):66 * (hh // 2) + 65],
                        S_f[lo:lo + 64, 66 * (hh // 2):66 * (hh // 2) + 65],
                        psB[lo:lo + 64, off:off + 65], op=ALU.add)),
                       waits=[("pe", pS)] if hh == 0 else [])
                    V_((lambda v, hh=hh, lo=lo: v.tensor_copy(
                        S_bf[lo:lo + 64, 66 * (hh // 2):66 * (hh // 2) + 65],
                        S_f[lo:lo + 64, 66 * (hh // 2):66 * (hh // 2) + 65])))

        def emit_stageB(ci, parity, pump):
            qfb, kfb, kTb, vab = qf2[parity], kf2[parity], kT2[parity], va2[parity]
            kfT_done_ac = marks[ci]["kfT_done_ac"]
            V_((lambda v: v.tensor_copy(
                AP(railsM, 2384, [[4096, 128], [64, 8], [1, 64]]),
                AP(railsM, 1856 + 64, [[4096, 128], [66, 8], [0, 64]]))),
               waits=[("pe", P.snap("pe"))])
            slot_dv = [P.snap("dv")] * 2
            for hh in range(H):
                if hh < 8:
                    pump()
                sl = hh % 2
                base = sl * 1024
                bb, lo = hh // 2, (hh % 2) * 64
                kwo = {} if sl == 0 else {"tile_position": (0, 64)}
                kws = {} if sl == 0 else {"tile_position": (64, 64)}
                kwa = {} if sl == 0 else {"tile_position": (64, 0)}
                ctx0 = base + 512
                den0 = base + 768
                dS0 = base + 384

                T_((lambda t, base=base, bb=bb, lo=lo, kwa=kwa, kfb=kfb, qfb=qfb:
                    t.matmul(
                        psB[:, base:base + 256],
                        kfb[lo:lo + 64, bb * 256:bb * 256 + 128],
                        qfb[lo:lo + 64, bb * 256:bb * 256 + 256],
                        start=True, stop=True, **kwa)),
                   waits=[("dv", slot_dv[sl]), ("ac", kfT_done_ac)])
                T_((lambda t, base=base, bb=bb, lo=lo, kwa=kwa, kfb=kfb, qfb=qfb:
                    t.matmul(
                        psB[:, base + 256:base + 384],
                        kfb[lo:lo + 64, bb * 256 + 128:bb * 256 + 256],
                        qfb[lo:lo + 64, bb * 256 + 128:bb * 256 + 256],
                        start=True, stop=True, **kwa)))
                pa = P.snap("pe")
                V_((lambda v, base=base: v.tensor_tensor(
                    ab_bf[:, 0:384], psB[:, base:base + 384],
                    mask384_bf[:, 0:384], op=ALU.mult)), waits=[("pe", pa)])
                da = P.snap("dv")

                T_((lambda t, ctx0=ctx0, hh=hh, lo=lo, bb=bb, kws=kws, qfb=qfb:
                    t.matmul(
                        psB[lo:lo + 64, ctx0:ctx0 + 256],
                        S_bf[lo:lo + 64, 66 * (hh // 2):66 * (hh // 2) + 64],
                        qfb[lo:lo + 64, bb * 256:bb * 256 + 256],
                        start=True, stop=False, **kws)), waits=[("dv", da)])
                T_((lambda t, ctx0=ctx0, hh=hh, lo=lo, kwo=kwo, vab=vab: t.matmul(
                    psB[lo:lo + 64, ctx0:ctx0 + 256],
                    vab[:, 66 * hh:66 * hh + 64],
                    ab_bf[:, 0:256], start=False, stop=False, **kwo)))
                T_((lambda t, ctx0=ctx0, hh=hh, lo=lo, kwo=kwo, vab=vab: t.matmul(
                    psB[lo:lo + 64, ctx0 + 128:ctx0 + 256],
                    vab[:, 1056 + 66 * hh:1056 + 66 * hh + 64],
                    ab_bf[:, 256:384], start=False, stop=True, **kwo)))
                T_((lambda t, den0=den0, hh=hh, lo=lo, bb=bb, kws=kws,
                    qfb=qfb: t.matmul(
                        psB[lo:lo + 64, den0:den0 + 256],
                        denrep[lo:lo + 64, bb * 64:bb * 64 + 64],
                        qfb[lo:lo + 64, bb * 256:bb * 256 + 256],
                        start=True, stop=False, **kws)))
                T_((lambda t, den0=den0, lo=lo, kwo=kwo: t.matmul(
                    psB[lo:lo + 64, den0:den0 + 256],
                    allones_bf[:, 0:64],
                    ab_bf[:, 0:256], start=False, stop=False, **kwo)))
                T_((lambda t, den0=den0, lo=lo, kwo=kwo: t.matmul(
                    psB[lo:lo + 64, den0 + 128:den0 + 256],
                    allones_bf[:, 0:64],
                    ab_bf[:, 256:384], start=False, stop=True, **kwo)))
                T_((lambda t, dS0=dS0, hh=hh, lo=lo, bb=bb, kwo=kwo, kTb=kTb,
                    vab=vab: t.matmul(
                        psB[lo:lo + 64, dS0:dS0 + 65],
                        kTb[:, (bb * 2) * 128 + lo:(bb * 2) * 128 + lo + 64],
                        vab[:, 66 * hh:66 * hh + 65],
                        start=True, stop=False, **kwo)))
                T_((lambda t, dS0=dS0, hh=hh, lo=lo, bb=bb, kwo=kwo, kTb=kTb,
                    vab=vab: t.matmul(
                        psB[lo:lo + 64, dS0:dS0 + 65],
                        kTb[:, (bb * 2 + 1) * 128 + lo:(bb * 2 + 1) * 128 + lo + 64],
                        vab[:, 1056 + 66 * hh:1056 + 66 * hh + 65],
                        start=False, stop=True, **kwo)))
                pc = P.snap("pe")

                V_((lambda v, den0=den0, lo=lo: v.tensor_scalar(
                    f32sB[lo:lo + 64, 0:256], psB[lo:lo + 64, den0:den0 + 256],
                    1e-6, None, op0=ALU.max)), waits=[("pe", pc)])
                dd = P.snap("dv")
                A_((lambda a, lo=lo: a.activation(
                    f32sB[lo:lo + 64, 0:256], f32sB[lo:lo + 64, 0:256], AF.Ln)),
                   waits=[("dv", dd)])
                A_((lambda a, lo=lo: a.activation(
                    dinv[lo:lo + 64, 0:256], f32sB[lo:lo + 64, 0:256], AF.Exp,
                    scale=-1.0)))
                ad = P.snap("ac")
                V_((lambda v, ctx0=ctx0, lo=lo, bb=bb: v.tensor_tensor(
                    ctx_bf[lo:lo + 64, bb * 256:(bb + 1) * 256],
                    psB[lo:lo + 64, ctx0:ctx0 + 256],
                    dinv[lo:lo + 64, 0:256], op=ALU.mult)), waits=[("ac", ad)])
                V_((lambda v, dS0=dS0, hh=hh, lo=lo: v.tensor_tensor(
                    S_f[lo:lo + 64, 66 * (hh // 2):66 * (hh // 2) + 65],
                    S_f[lo:lo + 64, 66 * (hh // 2):66 * (hh // 2) + 65],
                    psB[lo:lo + 64, dS0:dS0 + 65], op=ALU.add)))
                slot_dv[sl] = P.snap("dv")
            V_((lambda v: v.tensor_copy(S_bf[:, :], S_f[:, :])))
            scan_done = P.snap("dv")

            pump()
            for ob in range(8):
                for db in range(DB):
                    T_((lambda t, ob=ob, db=db: t.matmul(
                        psB[:, ob * 256:(ob + 1) * 256],
                        wout_bf[:, db * 1024 + ob * 128:db * 1024 + (ob + 1) * 128],
                        ctx_bf[:, db * 256:(db + 1) * 256],
                        start=(db == 0), stop=(db == DB - 1))),
                       waits=[("dv", scan_done)] if (ob == 0 and db == 0) else [])
            po = P.snap("pe")
            for ob in range(8):
                V_((lambda v, ob=ob, qfb=qfb: v.tensor_scalar(
                    qfb[:, ob * 256:(ob + 1) * 256],
                    psB[:, ob * 256:(ob + 1) * 256],
                    bout_sb[:, ob:ob + 1], None, op0=ALU.add)),
                   waits=[("pe", po)] if ob == 0 else [])
            pump()
            emit_ln(qfb, ctx_bf, psB, kfb, f32sB, mrB, resid_cols=ci * 256)
            pump()

        bufs = dict(x_sb=x_sb, svecs=svecs, ln1_bf=ln1_bf,
                    qk_bf=qk_bf, vaug=va2[0], qf_bf=qf2[0],
                    kf_bf=kf2[0], ctx=ctx_bf,
                    S_f=S_f, S_bf=S_bf, xa_bf=xa_bf, sq_bf=sq_bf, pn_bf=pn_bf,
                    wqkv_bf=wqkv_bf, omega_bf=omega_bf, mr_bf=mr_bf, f32s=f32s,
                    small_f=small_f, qnw_sb=qnw_sb)

        for ci in range(n_carry):
            for _ in stageA_gen(ci, True, 0):
                pass
        pend = []

        def pump():
            if pend:
                gen = pend[0]
                try:
                    next(gen)
                except StopIteration:
                    pend.pop(0)

        if n_main > 0:
            g0 = stageA_gen(0, False, 0)
            for _ in g0:
                pass
        for i in range(n_main):
            if i + 1 < n_main:
                pend.append(stageA_gen(i + 1, False, (i + 1) % 2))
            emit_stageB(i, i % 2, pump)
            while pend:
                pump()
        attn_done_dv = P.snap("dv")

        for bn in dumps:
            bt = bufs[bn]
            rows, cols = bt.shape
            dt_out = nc.dram_tensor("dump_" + bn, [rows, cols], F32,
                                    kind="ExternalOutput")
            dbg["dump_" + bn] = dt_out
            GP((lambda g, bt=bt, dt_out=dt_out, rows=rows, cols=cols: g.dma_start(
                AP(dt_out, 0, [[cols, rows], [1, cols]]),
                AP(bt, 0, [[cols, rows], [1, cols]]))),
               waits=[("dv", P.snap("dv")), ("ac", P.snap("ac")),
                      ("pe", P.snap("pe"))], inc="ddbg")

        if dbg.get("x_attn") is not None:
            SY((lambda s: s.dma_start(
                AP(dbg["x_attn"], 0, [[NOWN, 128], [NOWN * 128, 8], [1, NOWN]]),
                AP(x_sb, 0, [[DB * NOWN, 128], [NOWN, 8], [1, NOWN]]))),
               waits=[("dv", attn_done_dv)], inc="dout")

        if not do_ffn:
            SY((lambda s: s.dma_start(
                AP(y_out, 0, [[NOWN, 128], [NOWN * 128, 8], [1, NOWN]]),
                AP(x_sb, 0, [[DB * NOWN, 128], [NOWN, 8], [1, NOWN]]))),
               waits=[("dv", attn_done_dv)], inc="dout")
            n_out = P.snap("dout")
            SY(lambda s: s.nop(), waits=[("dout", n_out)])
            GP(lambda g: g.nop(), waits=[("ddbg", P.snap("ddbg"))])
            P.run(nc, sems)
            es_attn.close()
            return nc, dbg
        es_attn.close()
        with nc.sbuf_tensor("rmt_bf", [16, 4096], BF16) as rmt_bf, \
             nc.sbuf_tensor("rut_bf", [16, 16384], BF16) as rut_bf:
            GP(lambda g: g.dma_start(rmt_bf[:, :], rmt[:, :]),
               waits=[("dv", attn_done_dv)], inc="dw")
            for part in range(4):
                GP((lambda g, part=part: g.dma_start(
                    rut_bf[:, part * 4096:(part + 1) * 4096],
                    AP(rut, part * 4096, [[16384, 16], [1, 4096]]))), inc="dw")
            n_dw1 = P.snap("dw")
            V_(lambda v: v.memset(railsM[:, :], 0.0),
               waits=[("dv", attn_done_dv)])
            n_zero = P.snap("dv")
            for fc in range(128):
                h_, gp_ = fc // 32, fc % 32
                hr = h_ * 64 + 2 * gp_
                slot = fc % 4
                w = [("dw", n_dw1), ("dv", n_zero)] if fc == 0 else []
                if fc >= 4:
                    w = [("dv", n_zero + (fc - 4) + 1)]
                T_((lambda t, hr=hr, slot=slot: t.matmul(
                    psB[0:32, slot * 512:slot * 512 + 128],
                    rmt_bf[:, hr * 16:hr * 16 + 32],
                    rut_bf[:, hr * 64:hr * 64 + 128],
                    start=True, stop=True)), waits=w)
                q4, rg = fc // 4, fc % 4
                V_((lambda v, slot=slot, q4=q4, rg=rg: v.tensor_tensor(
                    railsM[rg * 32:(rg + 1) * 32, q4 * 128:(q4 + 1) * 128],
                    psB[0:32, slot * 512:slot * 512 + 128],
                    diagmask_bf[:, :], op=ALU.mult)),
                   waits=[("pe", P.snap("pe"))])
        rails_built_pe = P.snap("pe")
        rails_built_dv = P.snap("dv")
        ln3 = sb("ln3", (128, DB * 1024), BF16)
        wc0 = sb("wc0", (128, 8 * 1024), BF16)
        wc1 = sb("wc1", (128, 8 * 1024), BF16)
        hsl0 = sb("hsl0", (128, 16 * 512), BF16)
        hsl1 = sb("hsl1", (128, 16 * 512), BF16)
        hslb = [hsl0, hsl1]
        oacc = sb("oacc", (128, 8 * 1024), F32)
        wcbufs = [wc0, wc1]
        wc_free_dv = [rails_built_dv, rails_built_dv]
        hsl_free = [rails_built_pe, rails_built_pe]

        def emit_wc_load(si):
            buf = wcbufs[si % 2]
            GP((lambda g, si=si, buf=buf: g.dma_start(
                AP(buf, 0, [[8 * 1024, 128], [1024, 8], [1, 1024]]),
                AP(wcT, si * 1024 * 1024, [[1024, 128], [128 * 1024, 8], [1, 1024]]))),
               waits=[("dv", wc_free_dv[si % 2])], inc="dwc")
            return P.snap("dwc")

        for half in range(2):
            t0 = half * 1024
            for cc in range(4):
                V_((lambda v, cc=cc, t0=t0: v.tensor_copy(
                    ap3(xa_bf),
                    AP(x_sb, t0 + cc * 256, [[DB * NOWN, 128], [NOWN, 8], [1, 256]]))),
                   waits=[("dv", attn_done_dv), ("pe", rails_built_pe)]
                   if (half == 0 and cc == 0) else [])
                emit_ln(xa_bf, pn_bf, psA, sq_bf, f32s, mr_bf)
                V_((lambda v, cc=cc: v.tensor_copy(
                    AP(ln3, cc * 256, [[DB * 1024, 128], [1024, 8], [1, 256]]),
                    ap3(pn_bf))),
                   waits=[("pe", rails_built_pe)] if (half == 0 and cc == 0) else [])
            ln3_done = P.snap("dv")
            if FFN_UPTO < 2:
                continue

            wc_sem = [emit_wc_load(0), emit_wc_load(1)]
            seqs = [(grp, g) for grp in range(8) for g in range(2)]
            h_done_for = {}
            gelu_ac = []

            def emit_rail_one(i, fcl, first_w=()):
                grp, g = seqs[i]
                fc = grp * 16 + fcl
                gp_ = fc % 32
                q4, rg = fc // 4, fc % 4
                dbl, pbase = gp_ // 4, (gp_ % 4) * 32
                zsl = (i * 16 + fcl) % 2
                hb = hslb[i % 2]
                w = list(first_w)
                zi = i * 16 + fcl
                if zi >= 2:
                    w.append(("ac", gelu_ac[zi - 2]))
                T_((lambda t, q4=q4, rg=rg, dbl=dbl, pbase=pbase, g=g,
                    zsl=zsl: t.matmul(
                        psA[:, zsl * 512:(zsl + 1) * 512],
                        railsM[rg * 32:(rg + 1) * 32, q4 * 128:(q4 + 1) * 128],
                        ln3[pbase:pbase + 32,
                            dbl * 1024 + g * 512:dbl * 1024 + (g + 1) * 512],
                        start=True, stop=True,
                        tile_position=(rg * 32, 0))), waits=w)
                pz = P.snap("pe")
                wg = [("pe", pz)]
                if fcl == 0:
                    wg.append(("pe", hsl_free[i % 2]))
                A_((lambda a, fcl=fcl, zsl=zsl, hb=hb: a.activation(
                    hb[:, fcl * 512:(fcl + 1) * 512],
                    psA[:, zsl * 512:(zsl + 1) * 512], AF.Gelu)), waits=wg)
                gelu_ac.append(P.snap("ac"))
                if fcl == 15:
                    h_done_for[i] = P.snap("ac")

            fw0 = [("dv", ln3_done)]
            for fcl in range(16):
                emit_rail_one(0, fcl, fw0 if fcl == 0 else ())
            if FFN_UPTO >= 3:
                for i, (grp, g) in enumerate(seqs):
                    hb = hslb[i % 2]
                    for obg in range(2):
                        dv_guard = P.snap("dv")
                        for fcl in range(16):
                            if obg == 0 and i + 1 < len(seqs):
                                emit_rail_one(i + 1, fcl)
                            for ob in range(4):
                                obi = obg * 4 + ob
                                w = []
                                if fcl == 0 and ob == 0:
                                    w = [("ac", h_done_for[i]),
                                         ("dwc", max(wc_sem)),
                                         ("dv", dv_guard)]
                                T_((lambda t, fcl=fcl, ob=ob, obi=obi, hb=hb:
                                    t.matmul(
                                        psB[:, ob * 512:(ob + 1) * 512],
                                        wcbufs[fcl // 8][:,
                                            (fcl % 8) * 1024 + obi * 128:
                                            (fcl % 8) * 1024 + (obi + 1) * 128],
                                        hb[:, fcl * 512:(fcl + 1) * 512],
                                        start=(fcl == 0), stop=(fcl == 15))),
                                   waits=w)
                        pm = P.snap("pe")
                        if obg == 1:
                            hsl_free[i % 2] = pm
                        if grp == 0:
                            V_((lambda v, g=g, obg=obg: v.tensor_copy(
                                AP(oacc, obg * 4096 + g * 512,
                                   [[8 * 1024, 128], [1024, 4], [1, 512]]),
                                AP(psB, 0, [[2048, 128], [512, 4], [1, 512]]))),
                               waits=[("pe", pm), ("dout", P.snap("dout"))])
                        else:
                            V_((lambda v, g=g, obg=obg: v.tensor_tensor(
                                AP(oacc, obg * 4096 + g * 512,
                                   [[8 * 1024, 128], [1024, 4], [1, 512]]),
                                AP(oacc, obg * 4096 + g * 512,
                                   [[8 * 1024, 128], [1024, 4], [1, 512]]),
                                AP(psB, 0, [[2048, 128], [512, 4], [1, 512]]),
                                op=ALU.add)), waits=[("pe", pm)])
                    if g == 1:
                        wc_free_dv[0] = P.snap("dv")
                        wc_free_dv[1] = P.snap("dv")
                        if grp < 7:
                            wc_sem = [emit_wc_load(2 * (grp + 1)),
                                      emit_wc_load(2 * (grp + 1) + 1)]
            y_done = P.snap("dv")
            if FFN_UPTO < 4:
                continue

            for cc in range(4):
                V_((lambda v, cc=cc: v.tensor_copy(
                    ap3(xa_bf),
                    AP(oacc, cc * 256, [[8 * 1024, 128], [1024, 8], [1, 256]]))),
                   waits=[("dv", y_done)] if cc == 0 else [])
                emit_ln(xa_bf, pn_bf, psA, sq_bf, f32s, mr_bf)
                if FFN_UPTO < 5:
                    continue
                V_((lambda v, cc=cc, t0=t0: v.tensor_tensor(
                    AP(oacc, cc * 256, [[8 * 1024, 128], [1024, 8], [1, 256]]),
                    AP(x_sb, t0 + cc * 256, [[DB * NOWN, 128], [NOWN, 8], [1, 256]]),
                    ap3(pn_bf), op=ALU.add)))
                dfin = P.snap("dv")
                if FFN_UPTO < 6:
                    continue
                SY((lambda s, cc=cc, t0=t0: s.dma_start(
                    AP(y_out, t0 + cc * 256,
                       [[NOWN, 128], [NOWN * 128, 8], [1, 256]]),
                    AP(oacc, cc * 256, [[8 * 1024, 128], [1024, 8], [1, 256]]))),
                   waits=[("dv", dfin)], inc="dout")

        n_out = P.snap("dout")
        SY(lambda s: s.nop(), waits=[("dout", n_out)])
        GP(lambda g: g.nop(), waits=[("ddbg", P.snap("ddbg"))])

        P.run(nc, sems)

    return nc, dbg


_NC_CACHE = {}


def _get_nc(debug=()):
    key = tuple(sorted(debug))
    if key not in _NC_CACHE:
        _NC_CACHE[key] = build_nc(debug)
    return _NC_CACHE[key]


def stage_inputs(L0_attn_x, L0_attn_w_qkv, L0_attn_b_qkv, L0_attn_w_out,
                 L0_attn_b_out, L0_attn_omega, L0_attn_quad_nodes,
                 L0_attn_quad_weights, L0_ffn_rail_maker, L0_ffn_rail_up_proj,
                 L0_ffn_w_cproj):
    x = np.ascontiguousarray(np.asarray(L0_attn_x, dtype=np.float32))
    wqkvT = np.ascontiguousarray(np.asarray(L0_attn_w_qkv, np.float32).T)
    bqkv = np.asarray(L0_attn_b_qkv, np.float32)
    bqkv_c = np.ascontiguousarray(bqkv[:2048].reshape(16, 128).T)
    bqv_row = np.ascontiguousarray(bqkv[2048:].reshape(1, DM))
    woutT = np.ascontiguousarray(np.asarray(L0_attn_w_out, np.float32).T)
    bout_c = np.ascontiguousarray(
        np.asarray(L0_attn_b_out, np.float32).reshape(8, 128).T)
    omega2 = np.ascontiguousarray(
        np.asarray(L0_attn_omega, np.float32).transpose(2, 1, 0, 3).reshape(64, 1024))
    qn = np.asarray(L0_attn_quad_nodes, np.float32)
    qw = np.asarray(L0_attn_quad_weights, np.float32)
    ridx = (np.arange(128) // 32) % 2
    qnw_col = np.ascontiguousarray(np.stack([qn[ridx], qw[ridx]], axis=1))
    rmt = np.ascontiguousarray(
        np.asarray(L0_ffn_rail_maker, np.float32).transpose(3, 0, 1, 2).reshape(16, 4096))
    rut = np.ascontiguousarray(
        np.asarray(L0_ffn_rail_up_proj, np.float32).transpose(3, 0, 1, 2).reshape(16, 16384))
    wcT = np.ascontiguousarray(np.asarray(L0_ffn_w_cproj, np.float32).T)

    c_eye = np.eye(128, dtype=np.float32)
    c_tri = np.triu(np.ones((128, 128), np.float32))
    c_allones = np.ones((128, 128), np.float32)
    bo = np.zeros((128, 128), np.float32)
    bo[:64, :64] = 1.0
    bo[64:, 64:] = 1.0
    dmsk = np.zeros((32, 128), np.float32)
    dmsk[:16, :64] = 1.0
    dmsk[16:, 64:] = 1.0
    m384 = np.ones((128, 384), np.float32)
    m384[:, 0:128] = np.triu(np.ones((128, 128), np.float32))
    m384[:, 256:384] = np.triu(np.ones((128, 128), np.float32))

    shared = dict(wqkvT=wqkvT, bqkv_c=bqkv_c, bqv_row=bqv_row, woutT=woutT,
                  bout_c=bout_c, omega2=omega2, qnw_col=qnw_col, rmt=rmt, rut=rut,
                  wcT=wcT, c_eye=c_eye, c_tri=c_tri, c_allones=c_allones,
                  c_blockones=bo, c_diagmask=dmsk, c_mask384=m384)

    in_maps = []
    for core in range(8):
        b, half = core // 2, core % 2
        xb = np.ascontiguousarray(x[b].T)
        m = dict(shared)
        m["x_own"] = np.ascontiguousarray(xb[:, half * NOWN:(half + 1) * NOWN])
        if half == 1:
            m["x_carry"] = np.ascontiguousarray(xb[:, 0:NOWN])
            m["cscale"] = np.ones((128, 1), np.float32)
        else:
            m["x_carry"] = np.zeros((DM, NOWN), np.float32)
            m["cscale"] = np.zeros((128, 1), np.float32)
        in_maps.append(m)
    return in_maps


def kernel(L0_attn_x, L0_attn_w_qkv, L0_attn_b_qkv, L0_attn_w_out, L0_attn_b_out,
           L0_attn_omega, L0_attn_quad_nodes, L0_attn_quad_weights,
           L0_ffn_rail_maker, L0_ffn_rail_up_proj, L0_ffn_w_cproj, _debug=()):
    in_maps = stage_inputs(L0_attn_x, L0_attn_w_qkv, L0_attn_b_qkv, L0_attn_w_out,
                           L0_attn_b_out, L0_attn_omega, L0_attn_quad_nodes,
                           L0_attn_quad_weights, L0_ffn_rail_maker,
                           L0_ffn_rail_up_proj, L0_ffn_w_cproj)
    nc, dbg = _get_nc(_debug)
    res = run_bass_kernel_spmd(nc, in_maps, core_ids=list(range(8)))
    out = np.empty((B, T, DM), np.float32)
    for core in range(8):
        b, half = core // 2, core % 2
        out[b, half * NOWN:(half + 1) * NOWN, :] = res.results[core]["y_out"].T
    if _debug:
        return out, res
    return out
